# version 14
# baseline (speedup 1.0000x reference)
import numpy as np
from contextlib import ExitStack

import concourse.bass as bass
import concourse.mybir as mybir
from concourse.bass_utils import run_bass_kernel_spmd

F32 = mybir.dt.float32
BF16 = mybir.dt.bfloat16
AF = mybir.ActivationFunctionType
ALU = mybir.AluOpType
AX = mybir.AxisListType.X

D = 1024
DFF = 4096
NCH = 8
S = 2048
NSEQ = 4
G = 1024
NSUB = G // 128
C = 64
NE = 8
NQ = 4
FQ = DFF // NQ
TOK = NSEQ * S

LAUNCH_MODE = "fused"
DEBUG_TAP = None
DEBUG_GROUPS = None
INTERLEAVE_FFN = True

ENG = ("tensor", "vector", "scalar", "gpsimd", "sync")
DMAQ = {"sync": "qsync", "gpsimd": "qpool", "scalar": "qscal",
        "vector": "qvec", "tensor": "qtens"}
ALLQ = ["qsync", "qpool", "qscal", "qvec", "qtens", "qsync2"]
_BF16_TAPS = ("xn0", "xn1", "qh", "kh", "kb", "vb")



def _footprint(ap):
    t = ap.tensor
    shape = list(t.shape)
    rowlen = shape[-1]
    off = ap.offset
    p0 = off // rowlen
    c0 = off % rowlen
    pspan = 1
    cspan = 1
    for step, count in ap.ap:
        if count <= 1:
            continue
        if step > 0 and step % rowlen == 0:
            pspan += (step // rowlen) * (count - 1)
        else:
            cspan += step * (count - 1)
    name = t.name
    if "PSUM" in str(t.space):
        bank_elems = 2048 // (2 if ap.dtype == BF16 else 4)
        c0b = (c0 // bank_elems) * bank_elems
        c1b = ((c0 + cspan + bank_elems - 1) // bank_elems) * bank_elems
        return (name, 0, 128, c0b, c1b, True)
    return (name, p0, p0 + pspan, c0, c0 + cspan, False)


def _overlap(a, b):
    return (a[0] == b[0] and a[1] < b[2] and b[1] < a[2]
            and a[3] < b[4] and b[3] < a[4])


class Sched:
    def __init__(self, nc):
        self.nc = nc
        self.ops = []
        self.accesses = {}
        self.scope = "init"

    def op(self, engine, fn, reads=(), writes=(), dma=False, tag="", qtag=None):
        i = len(self.ops)
        rfp = [_footprint(a) for a in reads]
        wfp = [_footprint(a) for a in writes]
        deps = set()
        for fp in rfp:
            kinds = ("w", "r") if fp[5] else ("w",)
            for (oj, kind, fp2) in self.accesses.get(fp[0], ()):
                if kind in kinds and _overlap(fp, fp2):
                    deps.add(oj)
        for fp in wfp:
            for (oj, kind, fp2) in self.accesses.get(fp[0], ()):
                if _overlap(fp, fp2):
                    deps.add(oj)
        q = (qtag or DMAQ[engine]) if dma else None
        self.ops.append(dict(kind="op", i=i, eng=engine, fn=fn, deps=deps,
                             dma=q, inc=False, tag=tag, scope=self.scope))
        for fp in rfp:
            self.accesses.setdefault(fp[0], []).append((i, "r", fp))
        for fp in wfp:
            self.accesses.setdefault(fp[0], []).append((i, "w", fp))
        return i

    def join(self, tag=""):
        self.ops.append(dict(kind="join", i=len(self.ops), tag=tag))
        self.accesses.clear()

    def finalize(self):
        nc = self.nc
        ops = self.ops
        for o in ops:
            if o["kind"] == "op":
                for d in o["deps"]:
                    od = ops[d]
                    if od["dma"]:
                        continue
                    if od["eng"] == "tensor" and o["eng"] == "tensor":
                        continue
                    od["inc"] = True
        last = {}
        for o in ops:
            if o["kind"] == "op":
                if not o["dma"]:
                    last[o["eng"]] = o["i"]
            else:
                for e, oi in last.items():
                    ops[oi]["inc"] = True
                last = {}
        ecnt = {e: 0 for e in ENG}
        qcnt = {q: 0 for q in ALLQ}
        for o in ops:
            if o["kind"] != "op":
                continue
            if o["dma"]:
                qcnt[o["dma"]] += 1
                o["q_done"] = qcnt[o["dma"]] * 16
            if o["inc"]:
                ecnt[o["eng"]] += 1
                o["e_done"] = ecnt[o["eng"]]
            else:
                o["e_done"] = None
            o["e_seen"] = ecnt[o["eng"]]
        clocks = list(ENG) + list(ALLQ)
        guar = {e: {c: 0 for c in clocks} for e in ENG}
        cum_e = {e: 0 for e in ENG}
        cum_q = {q: 0 for q in ALLQ}
        for o in ops:
            if o["kind"] == "join":
                tgt = {**{e: cum_e[e] for e in ENG},
                       **{q: cum_q[q] * 16 for q in cum_q}}
                o["jtgt"] = tgt
                for e in ENG:
                    for c in clocks:
                        if guar[e][c] < tgt[c]:
                            guar[e][c] = tgt[c]
                continue
            e = o["eng"]
            waits = []
            for dd in sorted(o["deps"]):
                od = ops[dd]
                if od["dma"]:
                    c, v = od["dma"], cum_q[od["dma"]] * 16
                else:
                    c, v = od["eng"], od["e_done"]
                    if c == e == "tensor":
                        continue
                    if v is None:
                        continue
                if guar[e][c] < v:
                    waits.append((c, v))
                    guar[e][c] = v
                    for c2, v2 in od["cvec"].items():
                        if guar[e][c2] < v2:
                            guar[e][c2] = v2
            o["waits"] = waits
            cv = dict(guar[e])
            cv[e] = o["e_seen"]
            if o["dma"]:
                cum_q[o["dma"]] += 1
                cv[o["dma"]] = o["q_done"]
            o["cvec"] = cv
            if o["inc"]:
                cum_e[e] += 1
        with ExitStack() as stx:
            sems = {}
            for c in clocks:
                sems[c] = stx.enter_context(nc.semaphore(f"sem_{c}"))
            blk = stx.enter_context(nc.Block())
            for e in ENG:
                self._emit_engine(blk, e, sems)

    def _emit_engine(self, blk, e, sems):
        ops = self.ops
        nc = self.nc

        def body(eng):
            cur = [None]
            cm = [None]

            def setscope(s):
                if s == cur[0]:
                    return
                if cm[0] is not None:
                    cm[0].__exit__(None, None, None)
                    cm[0] = None
                if s is not None:
                    cm[0] = nc.named_scope(s)
                    cm[0].__enter__()
                cur[0] = s
            for o in ops:
                if o["kind"] == "join":
                    for c, v in o["jtgt"].items():
                        if v > 0:
                            eng.wait_ge(sems[c], v)
                    continue
                if o["eng"] != e:
                    continue
                setscope(o.get("scope"))
                for c, v in o["waits"]:
                    eng.wait_ge(sems[c], v)
                inst = o["fn"](eng)
                if o["dma"]:
                    inst = inst.then_inc(sems[o["dma"]], 16)
                if o["inc"]:
                    inst.then_inc(sems[e], 1)
            setscope(None)
        getattr(blk, e)(body)



def prep_core_inputs(inputs, core):
    x = np.ascontiguousarray(
        np.asarray(inputs["L0_attn_x"])[core * NSEQ:(core + 1) * NSEQ]
        .reshape(TOK, D))
    d = {"x": x}
    d["wqkvT"] = np.asarray(inputs["L0_attn_W_qkv"]).T
    d["bqkv"] = np.asarray(inputs["L0_attn_b_qkv"]).reshape(1, 3 * D)
    d["wgT"] = np.asarray(inputs["L0_attn_W_gate"]).T
    d["bg"] = np.asarray(inputs["L0_attn_b_gate"]).reshape(1, D)
    d["wiT"] = np.asarray(inputs["L0_ffn_wi"]).T
    d["bi"] = np.asarray(inputs["L0_ffn_bi"]).reshape(1, DFF)
    d["woT"] = np.asarray(inputs["L0_ffn_wo"]).T
    d["bo"] = np.asarray(inputs["L0_ffn_bo"]).reshape(1, D)
    d["lAT"] = np.asarray(inputs["L0_ffn_lora_A"]).transpose(0, 2, 1)
    d["lBT"] = np.asarray(inputs["L0_ffn_lora_B"]).transpose(0, 2, 1)
    d["gwT"] = np.asarray(inputs["L0_ffn_gate_w"]).T
    d["gb"] = np.asarray(inputs["L0_ffn_gate_b"]).reshape(1, NE)
    d["Wq"] = np.asarray(inputs["L1_attn_Wq"])
    d["Wk"] = np.asarray(inputs["L1_attn_Wk"])
    d["Wv"] = np.asarray(inputs["L1_attn_Wv"])
    d["Wb"] = np.asarray(inputs["L1_attn_Wb"])
    d["convw"] = np.stack([np.asarray(inputs["L1_attn_conv_q"])[:, 0, :],
                           np.asarray(inputs["L1_attn_conv_k"])[:, 0, :],
                           np.asarray(inputs["L1_attn_conv_v"])[:, 0, :]], axis=0)
    d["gcols"] = np.asarray(inputs["L1_attn_g"]).reshape(2, 128).T
    d["Wo"] = np.asarray(inputs["L1_attn_Wo"])
    d["u1"] = np.asarray(inputs["L1_ffn_U1"])
    d["v1"] = np.asarray(inputs["L1_ffn_V1"])
    d["b1cols"] = np.asarray(inputs["L1_ffn_b1"]).reshape(32, 128).T
    d["u2"] = np.asarray(inputs["L1_ffn_U2"])
    d["v2"] = np.asarray(inputs["L1_ffn_V2"])
    d["b2"] = np.asarray(inputs["L1_ffn_b2"]).reshape(1, D)
    m = np.zeros((C, NSEQ), dtype=np.float32)
    for si in range(NSEQ):
        m[:, si] = (np.arange(C) <= core * NSEQ + si).astype(np.float32)
    d["aimask"] = np.tile(m, (2, 1))
    return {k: np.ascontiguousarray(np.asarray(v, dtype=np.float32))
            for k, v in d.items()}


L0_ONLY_DROP = {"Wq", "Wk", "Wv", "Wb", "convw", "gcols", "Wo", "u1", "v1",
                "b1cols", "u2", "v2", "b2", "aimask"}
L1_ONLY_DROP = {"wqkvT", "bqkv", "wgT", "bg", "wiT", "bi", "woT", "bo",
                "lAT", "lBT", "gwT", "gb"}



class Ctx:
    pass


def OP(cx, eng, fn, reads=(), writes=(), tag=""):
    return cx.S.op(eng, fn, reads, writes, False, tag)


def DMA(cx, eng, out, in_, tag="", qtag=None):
    rd = [in_] if "DRAM" not in str(in_.tensor.space) else []
    wr = [out] if "DRAM" not in str(out.tensor.space) else []
    return cx.S.op(eng, lambda E, o=out, i=in_: E.dma_start(out=o, in_=i),
                   rd, wr, True, tag, qtag)


def MM(cx, o, l, r, start, stop, tp=None):
    if tp is None:
        OP(cx, "tensor", lambda E, o=o, l=l, r=r, s=start, p=stop:
            E.matmul(o, l, r, start=s, stop=p), [l, r], [o])
    else:
        OP(cx, "tensor", lambda E, o=o, l=l, r=r, s=start, p=stop, t=tp:
            E.matmul(o, l, r, start=s, stop=p, tile_position=t), [l, r], [o])


def TT_(cx, eng, o, a, b, op):
    OP(cx, eng, lambda E, o=o, a=a, b=b, op=op: E.tensor_tensor(o, a, b, op=op),
       [a, b], [o])


def TS_(cx, eng, o, a, s1, s2, op0, op1=None):
    rd = [a] + [s for s in (s1, s2) if hasattr(s, "tensor")]
    if op1 is None:
        OP(cx, eng, lambda E, o=o, a=a, s1=s1, s2=s2, op0=op0:
            E.tensor_scalar(o, a, s1, s2, op0=op0), rd, [o])
    else:
        OP(cx, eng, lambda E, o=o, a=a, s1=s1, s2=s2, op0=op0, op1=op1:
            E.tensor_scalar(o, a, s1, s2, op0=op0, op1=op1), rd, [o])


def STT(cx, eng, o, a, s, b, op0, op1):
    rd = [a, b] + ([s] if hasattr(s, "tensor") else [])
    OP(cx, eng, lambda E, o=o, a=a, s=s, b=b, op0=op0, op1=op1:
        E.scalar_tensor_tensor(o, a, s, b, op0=op0, op1=op1), rd, [o])


def CP(cx, eng, o, a):
    OP(cx, eng, lambda E, o=o, a=a: E.tensor_copy(o, a), [a], [o])


def ACT(cx, o, a, func, bias=0.0, scale=1.0, accum=None):
    rd = [a] + [s for s in (bias, scale) if hasattr(s, "tensor")]
    wr = [o] + ([accum] if accum is not None else [])
    if accum is None:
        OP(cx, "scalar", lambda E, o=o, a=a, f=func, b=bias, s=scale:
            E.activation(o, a, f, bias=b, scale=s), rd, wr)
    else:
        OP(cx, "scalar", lambda E, o=o, a=a, f=func, b=bias, s=scale, ac=accum:
            E.activation(o, a, f, bias=b, scale=s, accum_out=ac), rd, wr)


def TRN(cx, o, a, ident):
    OP(cx, "tensor", lambda E, o=o, a=a, i=ident: E.transpose(o, a, i),
       [a, ident], [o])


def RED(cx, o, a, op):
    OP(cx, "vector", lambda E, o=o, a=a, op=op: E.tensor_reduce(o, a, axis=AX, op=op),
       [a], [o])



def build_program(stages=("l0", "l1"), debug_tap=None):
    nc = bass.Bass("TRN2", target_bir_lowering=False)
    cx = Ctx()
    cx.nc = nc
    cx.S = Sched(nc)
    cx.tap = debug_tap
    import ml_dtypes

    dr = {}
    dr["x"] = nc.dram_tensor("x", [TOK, D], F32, kind="ExternalInput")
    inspecs = [("wqkvT", [D, 3 * D]), ("bqkv", [1, 3 * D]), ("wgT", [D, D]),
               ("bg", [1, D]), ("wiT", [D, DFF]), ("bi", [1, DFF]),
               ("woT", [DFF, D]), ("bo", [1, D]), ("lAT", [NE, D, 16]),
               ("lBT", [NE, 16, DFF]), ("gwT", [D, NE]), ("gb", [1, NE]),
               ("Wq", [D, D]), ("Wk", [D, D]), ("Wv", [D, D]), ("Wb", [D, 4]),
               ("convw", [3, D, 4]), ("gcols", [128, 2]), ("Wo", [D, D]),
               ("u1", [D, 256]), ("v1", [256, DFF]), ("b1cols", [128, 32]),
               ("u2", [DFF, 256]), ("v2", [256, D]), ("b2", [1, D]),
               ("aimask", [128, NSEQ])]
    drop = set()
    if stages == ("l0",):
        drop = L0_ONLY_DROP
    elif stages == ("l1",):
        drop = L1_ONLY_DROP
    for n, shp in inspecs:
        if n not in drop:
            dr[n] = nc.dram_tensor(n, shp, F32, kind="ExternalInput")
    dr["out"] = nc.dram_tensor("out", [TOK, D], F32, kind="ExternalOutput")
    if debug_tap:
        dt = BF16 if debug_tap in _BF16_TAPS else F32
        dr["dbg"] = nc.dram_tensor("dbg", [TOK, DFF], dt, kind="ExternalOutput")
    cx.dr = dr

    sc = {}
    if "l0" in stages:
        for n, shp in [("b_wqkvT", [D, 3 * D]), ("b_wgT", [D, D]),
                       ("b_wiT", [D, DFF]), ("b_woT", [DFF, D]),
                       ("b_gwT", [D, NE]), ("b_apad", [D, 512]),
                       ("b_bpad", [4, 128, DFF])]:
            sc[n] = nc.dram_tensor(n, shp, BF16, kind="Internal")
    if "l1" in stages:
        for n, shp in [("b_Wq", [D, D]), ("b_Wk", [D, D]), ("b_Wv", [D, D]),
                       ("b_wog", [D, D]), ("b_u1", [D, 256]),
                       ("b_v1", [256, DFF]), ("b_u2", [DFF, 256]),
                       ("b_v2", [256, D])]:
            sc[n] = nc.dram_tensor(n, shp, BF16, kind="Internal")
    cx.sc = sc

    inl = {}
    inl["ident_f"] = nc.inline_tensor(np.eye(128, dtype=np.float32), name="i_idf")
    inl["ident_b"] = nc.inline_tensor(np.eye(128, dtype=ml_dtypes.bfloat16),
                                      name="i_idb")
    inl["utones"] = nc.inline_tensor(np.triu(np.ones((128, 128), np.float32)),
                                     name="i_ut")
    inl["masksl"] = nc.inline_tensor(
        np.tril(np.ones((C, C), np.float32), -1), name="i_msl")
    inl["twoI"] = nc.inline_tensor(
        np.tile(2.0 * np.eye(C), (2, 1)).astype(np.float32), name="i_2I")
    inl["ones_r"] = nc.inline_tensor(np.ones((1, 128), np.float32), name="i_1r")
    inl["ones_rb"] = nc.inline_tensor(np.ones((1, 128), ml_dtypes.bfloat16),
                                      name="i_1rb")
    inl["two_rb"] = nc.inline_tensor(np.full((1, 128), 2.0, ml_dtypes.bfloat16),
                                     name="i_2rb")
    inl["ones_c"] = nc.inline_tensor(np.ones((128, 1), np.float32), name="i_1c")
    inl["ones_cb"] = nc.inline_tensor(np.ones((128, 1), ml_dtypes.bfloat16),
                                      name="i_1cb")
    inl["utones_b"] = nc.inline_tensor(
        np.triu(np.ones((128, 128))).astype(ml_dtypes.bfloat16), name="i_utb")
    cx.inl = inl

    ARENA = 208448
    arena = nc.alloc_sbuf_tensor("my_arena", [128, ARENA], mybir.dt.uint8)
    base = [None]
    off = [None]

    def alloc(name, shape, dtype, align=64):
        if off[0] is None:
            mloc = nc.lookup_mloc(arena)
            base[0] = mloc.addr
            off[0] = base[0]
        off[0] = (off[0] + align - 1) // align * align
        t = nc.alloc_sbuf_tensor_at(name, list(shape), dtype, offset=off[0])
        off[0] += shape[-1] * (2 if dtype == BF16 else 4)
        return t

    cx.x_sb = alloc("x_sb", [128, NSUB * D], F32)
    cx.xn_sb = alloc("xn_sb", [128, NCH * G], BF16)
    cx.ident_f = alloc("c_idf", [128, 128], F32)
    cx.ident_b = alloc("c_idb", [128, 128], BF16)
    cx.utones = alloc("c_ut", [128, 128], F32)
    cx.masksl = alloc("c_msl", [C, C], F32)
    cx.twoI = alloc("c_2I", [128, C], F32)
    cx.ones_r = alloc("c_1r", [1, 128], F32)
    cx.ones_rb = alloc("c_1rb", [1, 128], BF16)
    cx.two_rb = alloc("c_2rb", [1, 128], BF16)
    cx.ones_c = alloc("c_1c", [128, 1], F32)
    cx.ones_cb = alloc("c_1cb", [128, 1], BF16)
    cx.utones_b = alloc("c_utb", [128, 128], BF16)
    cx.convw = alloc("c_cw", [128, 96], F32)
    cx.gcols = alloc("c_gc", [128, 2], F32)
    cx.b1cols = alloc("c_b1c", [128, 32], F32)
    cx.aimask = alloc("c_aim", [128, NSEQ], F32)
    cx.w_wb = alloc("c_wb", [128, 32], BF16)
    cx.tails = alloc("c_tails", [128, 72], F32)
    cx.cvals = alloc("c_cvals", [128, 4], F32)
    cx.r_bqkv = alloc("r_bqkv", [1, 3 * D], BF16)
    cx.r_bg = alloc("r_bg", [1, D], BF16)
    cx.r_gb = alloc("r_gb", [1, NE], BF16)
    cx.r_bi = alloc("r_bi", [1, DFF], BF16)
    cx.r_bo = alloc("r_bo", [1, D], BF16)
    cx.r_b2 = alloc("r_b2", [1, D], BF16)
    cx.r_carry = alloc("r_carry", [1, D], F32)
    cx.r_sd = alloc("r_sd", [1, D], F32)
    cx.r_sdb = alloc("r_sdb", [1, D], BF16)
    cx.r_rkb = alloc("r_rkb", [1, 4 * 512], BF16)
    cx.r_tmpf = alloc("r_tmpf", [1, 512], F32)
    cx.vs_g = alloc("vs_g", [128, NSUB], F32)
    cx.count_g = alloc("count_g", [128, NSUB * NE], F32)
    cx.qh = alloc("qh", [128, NCH * 512], BF16)
    cx.kh = alloc("kh", [128, NCH * 512], BF16)
    cx.kb = alloc("kb", [128, NCH * 512], BF16)
    cx.vb = alloc("vb", [128, NCH * 512], BF16)

    oA = (off[0] + 63) // 64 * 64
    oB = oA + 49152
    endB = oB + 32832
    assert endB <= base[0] + 208448, f"SBUF over budget: {endB - base[0]}"

    def allocA(name, shape, dtype, o):
        sz = shape[-1] * (2 if dtype == BF16 else 4)
        assert o + sz <= 49152, (name, o, sz)
        return nc.alloc_sbuf_tensor_at(name, list(shape), dtype, offset=oA + o)

    def allocB(name, shape, dtype, o):
        sz = shape[-1] * (2 if dtype == BF16 else 4)
        assert o + sz <= 32832, (name, o, sz)
        return nc.alloc_sbuf_tensor_at(name, list(shape), dtype, offset=oB + o)

    cx.pro_f = allocA("pro_f", [128, DFF], F32, 0)
    cx.pro_b = allocA("pro_b", [128, DFF], BF16, 16384)
    cx.w_qkvT = allocA("w_qkvT", [128, NCH * 3 * D], BF16, 0)
    cx.w_gT = allocB("w_gT", [128, NCH * D], BF16, 0)
    cx.q_f = allocB("q_f", [128, D], F32, 16384)
    cx.kv_b = allocB("kv_b", [128, D], BF16, 20480)
    cx.gate_f = allocB("gate_f", [128, D], F32, 24576)
    cx.t1_f = allocB("t1_f", [128, D], F32, 28672)
    cx.ln_tm = allocB("ln_tm", [128, D], BF16, 16384)
    cx.ln_sq = allocB("ln_sq", [128, D], F32, 20480)
    cx.ln_c = allocB("ln_c", [128, 64], F32, 32512)
    cx.w_apad = allocA("w_apad", [128, NCH * 512], BF16, 0)
    cx.w_gwT = allocA("w_gwT", [128, NCH * NE], BF16, 8192)
    cx.moe_sc = allocA("moe_sc", [128, 40], F32, 8448)
    cx.w_wiTq = allocA("w_wiTq", [128, NCH * FQ], BF16, 0)
    cx.w_bpadq = allocA("w_bpadq", [128, 4 * FQ], BF16, 16384)
    cx.w_woTq = allocA("w_woTq", [128, NCH * D], BF16, 24576)
    cx.tsum_b = [allocA("tsum0", [128, FQ], BF16, 40960),
                 allocA("tsum1", [128, FQ], BF16, 43008),
                 allocA("tsum2", [128, FQ], BF16, 45056),
                 allocA("tsum3", [128, FQ], BF16, 47104)]
    cx.relu_b2 = allocB("relu_b2", [128, FQ], BF16, 18432)
    cx.base_b = allocB("base_b", [128, FQ], BF16, 0)
    cx.base_b2 = allocB("base_b2", [128, FQ], BF16, 26624)
    cx.relu_b = allocB("relu_b", [128, FQ], BF16, 2048)
    cx.acc0_b = allocB("acc0_b", [128, FQ], BF16, 4096)
    cx.acc1_b = allocB("acc1_b", [128, FQ], BF16, 6144)
    cx.acc2_b = allocB("acc2_b", [128, FQ], BF16, 28672)
    cx.acc3_b = allocB("acc3_b", [128, FQ], BF16, 30720)
    cx.accfm_b = allocB("accfm_b", [128, FQ], BF16, 8192)
    cx.w_prj2 = [allocA("w_prj0", [128, NCH * D], BF16, 0),
                 allocA("w_prjB", [128, NCH * D], BF16, 24576)]
    cx.silu_f = allocA("silu_f", [128, NCH * 512], BF16, 16384)
    cx.aug_f = allocA("aug_f", [128, 520], F32, 40960)
    cx.csA_f = allocA("csA_f", [128, 512], F32, 43072)
    cx.csB_f = allocA("csB_f", [128, 512], F32, 45120)
    cx.sq_b = allocA("sq_b", [128, 512], BF16, 47168)
    cx.r_beta = allocB("r_beta", [1, 4 * 512], BF16, 18432)
    cx.r_rq = allocB("r_rq", [1, 4 * 512], BF16, 22528)
    cx.w_wog = allocB("w_wog", [128, NCH * D], BF16, 0)
    cx.tm_set = [
        (allocA("tm_qh0", [128, D], BF16, 0), allocA("tm_kh0", [128, D], BF16, 2048),
         allocA("tm_kb0", [128, D], BF16, 4096), allocA("tm_vb0", [128, D], BF16, 6144)),
        (allocB("tm_qh1", [128, D], BF16, 18432), allocB("tm_kh1", [128, D], BF16, 20480),
         allocB("tm_kb1", [128, D], BF16, 22528), allocB("tm_vb1", [128, D], BF16, 24576)),
    ]
    cx.WU_set = [
        (allocA("W_b0", [128, D], BF16, 8192), allocA("U_b0", [128, D], BF16, 10240)),
        (allocA("W_b1", [128, D], BF16, 12288), allocA("U_b1", [128, D], BF16, 14336)),
    ]
    cx.o_set = [allocA("o_tm0", [128, D], F32, 28672),
                allocB("o_tm1", [128, D], F32, 26624)]
    cx.u_f = allocA("u_f", [128, D], F32, 16384)
    cx.t3_f = allocA("t3_f", [128, D], F32, 20480)
    cx.t3_b = allocB("t3_b", [128, D], BF16, 16384)
    cx.oi_f = allocA("oi_f", [128, D], F32, 24576)
    cx.osq_f = allocA("osq_f", [128, D], F32, 32768)
    cx.on_b = allocA("on_b", [128, D], BF16, 36864)
    cx.ofm_b = allocA("ofm_b", [128, D], BF16, 38912)
    cx.Lm = allocA("Lm", [C, C], F32, 40960)
    cx.LmT = allocA("LmT", [C, C], F32, 41216)
    cx.X_a = allocA("X_a", [C, C], F32, 41472)
    cx.XT_a = allocA("XT_a", [C, C], F32, 41728)
    cx.X_b = allocA("X_b", [C, C], F32, 41984)
    cx.XT_b = allocA("XT_b", [C, C], F32, 42240)
    cx.G_a = allocA("G_a", [C, C], F32, 42496)
    cx.G_b = allocA("G_b", [C, C], F32, 42752)
    cx.G_c = allocA("G_c", [C, C], F32, 43008)
    cx.TT4 = [allocA(f"TT4_{i}", [128, C], BF16, 43264 + 128 * i) for i in range(4)]
    cx.aq_c = allocA("aq_c", [128, 4], F32, 43840)
    cx.ou_f = allocA("ou_f", [128, D], F32, 43904)
    cx.rms_c = allocA("rms_c", [128, 12], F32, 48000)
    cx.w_u1 = allocA("w_u1", [128, NCH * 256], BF16, 0)
    cx.w_v1 = allocA("w_v1", [128, 2 * DFF], BF16, 4096)
    cx.w_u2 = allocA("w_u2", [128, 32 * 256], BF16, 20480)
    cx.w_v2 = allocA("w_v2", [128, 2 * D], BF16, 36864)
    cx.hg = allocB("hg", [128, 4 * G], BF16, 10240)
    cx.P_b = allocB("P_b", [128, 2 * G], BF16, 0)
    cx.H_b = allocB("H_b", [128, G], BF16, 4096)
    cx.Q_b = allocB("Q_b", [128, 2 * G], BF16, 6144)

    cx.ps_a = nc.alloc_psum_tensor("ps_a", [128, 2048], F32)
    cx.ps_b = nc.alloc_psum_tensor("ps_b", [128, 1024], F32)
    cx.ps_c = nc.alloc_psum_tensor("ps_c", [128, 512], F32)
    cx.ps_t = nc.alloc_psum_tensor("ps_t", [128, 1024], BF16)

    emit_consts(cx, stages)
    emit_prologue(cx, stages)
    cx.S.join("prologue")
    for si in range(NSEQ):
        for gh in range(2):
            if DEBUG_GROUPS is not None and (si, gh) not in DEBUG_GROUPS:
                continue
            emit_group(cx, stages, si, gh)
    cx.S.join("end")
    cx.S.finalize()
    return nc


def emit_consts(cx, stages):
    d = cx.dr
    for sb, key in [(cx.ident_f, "ident_f"), (cx.ident_b, "ident_b"),
                    (cx.utones, "utones"), (cx.masksl, "masksl"),
                    (cx.twoI, "twoI"), (cx.ones_r, "ones_r"),
                    (cx.ones_rb, "ones_rb"), (cx.two_rb, "two_rb"),
                    (cx.ones_c, "ones_c"), (cx.ones_cb, "ones_cb"),
                    (cx.utones_b, "utones_b")]:
        DMA(cx, "sync", sb.ap()[:, :], cx.inl[key].ap()[:, :], "const")
    OP(cx, "vector", lambda E: E.memset(cx.cvals.ap()[:, 0:1], 1e-5),
       [], [cx.cvals.ap()[:, 0:1]])
    for name, sb in [("gcols", cx.gcols), ("b1cols", cx.b1cols),
                     ("aimask", cx.aimask)]:
        if name in d:
            DMA(cx, "sync", sb.ap()[:, :], d[name].ap()[:, :])
    if "convw" in d:
        for t in range(3):
            for c in range(NCH):
                o0 = t * 32 + c * 4
                DMA(cx, "sync", cx.convw.ap()[:, o0:o0 + 4],
                    d["convw"].ap()[t, c * 128:(c + 1) * 128, :])


def emit_prologue(cx, stages):
    cx.S.scope = "prologue"
    d, sc = cx.dr, cx.sc
    state = {"i": 0}

    def cast_weight(src_, dst, rows, cols):
        for r0 in range(0, rows, 128):
            i = state["i"] % 2
            state["i"] += 1
            pf = cx.pro_f.ap()[:, i * 2048:i * 2048 + cols] if cols <= 2048 else \
                cx.pro_f.ap()[:, 0:cols]
            pb = cx.pro_b.ap()[:, i * 2048:i * 2048 + cols] if cols <= 2048 else \
                cx.pro_b.ap()[:, 0:cols]
            rr = min(128, rows - r0)
            DMA(cx, "sync", pf[0:rr, :], src_.ap()[r0:r0 + rr, :])
            if i == 0:
                CP(cx, "vector", pb[0:rr, :], pf[0:rr, :])
            else:
                ACT(cx, pb[0:rr, :], pf[0:rr, :], AF.Copy)
            DMA(cx, "gpsimd", dst.ap()[r0:r0 + rr, :], pb[0:rr, :])

    def cast_row(src_, dst, n):
        pf = cx.pro_f.ap()
        DMA(cx, "sync", pf[0:1, 0:n], src_.ap()[:, :])
        CP(cx, "vector", dst.ap()[0:1, 0:n], pf[0:1, 0:n])

    if "l0" in stages:
        cast_weight(d["wqkvT"], sc["b_wqkvT"], D, 3 * D)
        cast_weight(d["wgT"], sc["b_wgT"], D, D)
        cast_weight(d["wiT"], sc["b_wiT"], D, DFF)
        cast_weight(d["woT"], sc["b_woT"], DFF, D)
        cast_weight(d["gwT"], sc["b_gwT"], D, NE)
        cast_row(d["bqkv"], cx.r_bqkv, 3 * D)
        cast_row(d["bg"], cx.r_bg, D)
        cast_row(d["gb"], cx.r_gb, NE)
        cast_row(d["bi"], cx.r_bi, DFF)
        cast_row(d["bo"], cx.r_bo, D)
        for c in range(NCH):
            pb = cx.pro_b.ap()
            OP(cx, "vector", lambda E, pb=pb: E.memset(pb[:, 0:512], 0.0),
               [], [pb[:, 0:512]])
            for p in range(4):
                for el in range(2):
                    e = 2 * p + el
                    dst0 = 128 * p + 64 * el
                    DMA(cx, "sync", cx.pro_f.ap()[:, 0:16],
                        d["lAT"].ap()[e, c * 128:(c + 1) * 128, :])
                    CP(cx, "vector", pb[:, dst0:dst0 + 16], cx.pro_f.ap()[:, 0:16])
            DMA(cx, "gpsimd", sc["b_apad"].ap()[c * 128:(c + 1) * 128, :],
                pb[:, 0:512])
        for p in range(4):
            pb = cx.pro_b.ap()
            OP(cx, "vector", lambda E, pb=pb: E.memset(pb[:, :], 0.0),
               [], [pb[:, :]])
            for el in range(2):
                e = 2 * p + el
                DMA(cx, "sync", cx.pro_f.ap()[0:16, :], d["lBT"].ap()[e, :, :])
                CP(cx, "vector", pb[64 * el:64 * el + 16, :], cx.pro_f.ap()[0:16, :])
            DMA(cx, "gpsimd", sc["b_bpad"].ap()[p, :, :], pb[:, :])
    if "l1" in stages:
        cast_weight(d["Wq"], sc["b_Wq"], D, D)
        cast_weight(d["Wk"], sc["b_Wk"], D, D)
        cast_weight(d["Wv"], sc["b_Wv"], D, D)
        cast_weight(d["u1"], sc["b_u1"], D, 256)
        cast_weight(d["v1"], sc["b_v1"], 256, DFF)
        cast_weight(d["u2"], sc["b_u2"], DFF, 256)
        cast_weight(d["v2"], sc["b_v2"], 256, D)
        cast_row(d["b2"], cx.r_b2, D)
        for c in range(NCH):
            i = c % 2
            pf = cx.pro_f.ap()[:, i * 2048:i * 2048 + D]
            pb = cx.pro_b.ap()[:, i * 2048:i * 2048 + D]
            DMA(cx, "sync", pf, d["Wo"].ap()[c * 128:(c + 1) * 128, :])
            TS_(cx, "vector", pb, pf,
                cx.gcols.ap()[:, c % 2:c % 2 + 1], None, ALU.mult)
            DMA(cx, "gpsimd", sc["b_wog"].ap()[c * 128:(c + 1) * 128, :], pb)
        for c in range(NCH):
            DMA(cx, "sync", cx.pro_f.ap()[:, 0:4], d["Wb"].ap()[c * 128:(c + 1) * 128, :])
            CP(cx, "vector", cx.w_wb.ap()[:, 4 * c:4 * c + 4], cx.pro_f.ap()[:, 0:4])



def emit_group(cx, stages, si, gh):
    gid = si * 2 + gh
    t0 = si * S + gh * G
    xv = cx.dr["x"].ap().rearrange("(a p) d -> p a d", p=128)
    DMA(cx, "gpsimd", cx.x_sb.ap().rearrange("p (a d) -> p a d", d=D),
        xv[:, t0 // 128:t0 // 128 + NSUB, :], "ldx")
    if gh == 0:
        OP(cx, "vector", lambda E: E.memset(cx.r_carry.ap()[:, :], 0.0),
           [], [cx.r_carry.ap()[:, :]])
        OP(cx, "vector", lambda E: E.memset(cx.r_sd.ap()[:, :], 0.0),
           [], [cx.r_sd.ap()[:, :]])
        OP(cx, "vector", lambda E: E.memset(cx.tails.ap()[:, :], 0.0),
           [], [cx.tails.ap()[:, :]])
    cx.S.join(f"ldx{si}{gh}")
    if cx.tap == "x0":
        tap_tm(cx, t0)
        cx.S.join("tapx0")
    if "l0" in stages:
        cx.S.scope = f"ln0_g{gid}"
        emit_ln(cx, t0, "xn0")
        cx.S.join("ln0")
        cx.S.scope = f"l0attn_g{gid}"
        emit_l0attn(cx, si, gh, t0)
        cx.S.join("l0attn")
        cx.S.scope = f"moe_g{gid}"
        emit_moe_gate(cx, t0)
        cx.S.join("moe")
        cx.S.scope = f"l0ffn_g{gid}"
        emit_l0ffn(cx, t0)
        if cx.tap == "x1":
            tap_tm(cx, t0)
            cx.S.join("tapx1")
    if "l1" in stages:
        cx.S.scope = f"ln1_g{gid}"
        emit_ln(cx, t0, "xn1")
        cx.S.join("ln1")
        emit_l1attn(cx, si, gh, t0)
        cx.S.scope = f"l1ffn_g{gid}"
        emit_l1ffn(cx, t0)
        cx.S.join("l1ffn")
        if cx.tap == "x2":
            tap_tm(cx, t0)
            cx.S.join("tapx2")
    ov = cx.dr["out"].ap().rearrange("(a p) d -> p a d", p=128)
    DMA(cx, "gpsimd", ov[:, t0 // 128:t0 // 128 + NSUB, :],
        cx.x_sb.ap().rearrange("p (a d) -> p a d", d=D), "stx")
    cx.S.join(f"stx{si}{gh}")


def tap_tm(cx, t0):
    dv = cx.dr["dbg"].ap().rearrange("(a p) d -> a p d", p=128)
    for s in range(NSUB):
        DMA(cx, "gpsimd", dv[(t0 + s * 128) // 128, :, 0:D],
            cx.x_sb.ap()[:, s * D:(s + 1) * D], "tap")


def tap_fm_bf16(cx, t0, sb, width):
    dv = cx.dr["dbg"].ap().rearrange("(a p) d -> a p d", p=128)
    for c in range(NCH):
        DMA(cx, "gpsimd", dv[(t0 + c * 128) // 128, :, 0:width],
            sb.ap()[:, c * width:(c + 1) * width], "tapf")


def emit_ln(cx, t0, tapname):
    lc = cx.ln_c.ap()
    OP(cx, "vector", lambda E: E.memset(lc[:, 8:16], 0.0), [], [lc[:, 8:16]])
    for s in range(NSUB):
        xs = cx.x_sb.ap()[:, s * D:(s + 1) * D]
        RED(cx, lc[:, s:s + 1], xs, ALU.add)
        ACT(cx, cx.ln_sq.ap()[:, :], xs, AF.Square, accum=lc[:, 8 + s:9 + s])
    TS_(cx, "vector", lc[:, 16:24], lc[:, 0:8], 1.0 / D, None, ALU.mult)
    TS_(cx, "vector", lc[:, 24:32], lc[:, 8:16], 1.0 / D, None, ALU.mult)
    TT_(cx, "vector", lc[:, 32:40], lc[:, 16:24], lc[:, 16:24], ALU.mult)
    TT_(cx, "vector", lc[:, 40:48], lc[:, 24:32], lc[:, 32:40], ALU.subtract)
    ACT(cx, lc[:, 48:56], lc[:, 40:48], AF.Ln, bias=cx.cvals.ap()[:, 0:1])
    ACT(cx, lc[:, 56:64], lc[:, 48:56], AF.Exp, scale=-0.5)
    for s in range(NSUB):
        xs = cx.x_sb.ap()[:, s * D:(s + 1) * D]
        TS_(cx, "vector", cx.ln_tm.ap()[:, :], xs, lc[:, 16 + s:17 + s],
            lc[:, 56 + s:57 + s], ALU.subtract, ALU.mult)
        for c in range(NCH):
            TRN(cx, cx.ps_t.ap()[:, c * 128:(c + 1) * 128],
                cx.ln_tm.ap()[:, c * 128:(c + 1) * 128], cx.ident_b.ap()[:, :])
        dst = cx.xn_sb.ap().rearrange("p (c g) -> p c g", g=G)[:, :, s * 128:(s + 1) * 128]
        OP(cx, "vector", lambda E, dst=dst: E.tensor_copy(
            dst, cx.ps_t.ap().rearrange("p (c n) -> p c n", n=128)),
            [cx.ps_t.ap()[:, :]], [dst])
    if cx.tap == tapname:
        tap_fm_bf16(cx, t0, cx.xn_sb, G)


def emit_l0attn(cx, si, gh, t0):
    for c in range(NCH):
        DMA(cx, "sync", cx.w_qkvT.ap()[:, c * 3 * D:(c + 1) * 3 * D],
            cx.sc["b_wqkvT"].ap()[c * 128:(c + 1) * 128, :])
        DMA(cx, "sync", cx.w_gT.ap()[:, c * D:(c + 1) * D],
            cx.sc["b_wgT"].ap()[c * 128:(c + 1) * 128, :])
    for s in range(NSUB):
        xn_s = [cx.xn_sb.ap()[:, c * G + s * 128: c * G + (s + 1) * 128]
                for c in range(NCH)]
        def psn(n):
            return (cx.ps_a.ap()[:, n * 512:(n + 1) * 512] if n < 4
                    else cx.ps_b.ap()[:, (n - 4) * 512:(n - 3) * 512])
        for c in range(NCH):
            for n in range(6):
                w = cx.w_qkvT.ap()[:, c * 3 * D + n * 512: c * 3 * D + (n + 1) * 512]
                MM(cx, psn(n), xn_s[c], w, c == 0, False)
        for n in range(6):
            br = cx.r_bqkv.ap()[0:1, n * 512:(n + 1) * 512]
            MM(cx, psn(n), cx.ones_rb.ap()[:, :], br, False, True)
        ACT(cx, cx.q_f.ap()[:, :], cx.ps_a.ap()[:, 0:1024], AF.Copy)
        ACT(cx, cx.t1_f.ap()[:, :], cx.ps_a.ap()[:, 1024:2048], AF.Copy)
        TT_(cx, "vector", cx.kv_b.ap()[:, :], cx.t1_f.ap()[:, :],
            cx.ps_b.ap()[:, :], ALU.mult)
        for c in range(NCH):
            for n in range(2):
                w = cx.w_gT.ap()[:, c * D + n * 512: c * D + (n + 1) * 512]
                MM(cx, cx.ps_b.ap()[:, n * 512:(n + 1) * 512], xn_s[c], w,
                   c == 0, False)
        for n in range(2):
            br = cx.r_bg.ap()[0:1, n * 512:(n + 1) * 512]
            MM(cx, cx.ps_b.ap()[:, n * 512:(n + 1) * 512],
               cx.ones_rb.ap()[:, :], br, False, True)
        ACT(cx, cx.gate_f.ap()[:, :], cx.ps_b.ap()[:, :], AF.Sigmoid)
        for n in range(2):
            ps = cx.ps_a.ap()[:, 1024 + n * 512: 1024 + (n + 1) * 512]
            kvs = cx.kv_b.ap()[:, n * 512:(n + 1) * 512]
            MM(cx, ps, cx.utones_b.ap()[:, :], kvs, True, False)
            cr = cx.r_carry.ap()[0:1, n * 512:(n + 1) * 512]
            MM(cx, ps, cx.ones_r.ap()[:, :], cr, False, True)
        for n in range(2):
            ps = cx.ps_c.ap()[0:1, 0:512]
            kvs = cx.kv_b.ap()[:, n * 512:(n + 1) * 512]
            MM(cx, ps, cx.ones_cb.ap()[:, :], kvs, True, True)
            cr = cx.r_carry.ap()[0:1, n * 512:(n + 1) * 512]
            TT_(cx, "vector", cr, cr, ps, ALU.add)
        TT_(cx, "vector", cx.t1_f.ap()[:, :], cx.q_f.ap()[:, :],
            cx.ps_a.ap()[:, 1024:2048], ALU.mult)
        TT_(cx, "vector", cx.q_f.ap()[:, :], cx.t1_f.ap()[:, :],
            cx.gate_f.ap()[:, :], ALU.mult)
        xs = cx.x_sb.ap()[:, s * D:(s + 1) * D]
        TT_(cx, "vector", xs, xs, cx.q_f.ap()[:, :], ALU.add)
    if cx.tap == "a0x":
        tap_tm(cx, t0)
        cx.S.join("tapa0")


def emit_moe_gate(cx, t0):
    for c in range(NCH):
        DMA(cx, "sync", cx.w_gwT.ap()[:, c * NE:(c + 1) * NE],
            cx.sc["b_gwT"].ap()[c * 128:(c + 1) * 128, :])
        DMA(cx, "sync", cx.w_apad.ap()[:, c * 512:(c + 1) * 512],
            cx.sc["b_apad"].ap()[c * 128:(c + 1) * 128, :])
    ms = cx.moe_sc.ap()
    for s in range(NSUB):
        xn_s = [cx.xn_sb.ap()[:, c * G + s * 128: c * G + (s + 1) * 128]
                for c in range(NCH)]
        ps = cx.ps_c.ap()[:, 0:NE]
        for c in range(NCH):
            MM(cx, ps, xn_s[c], cx.w_gwT.ap()[:, c * NE:(c + 1) * NE],
               c == 0, False)
        MM(cx, ps, cx.ones_rb.ap()[:, :], cx.r_gb.ap()[:, :], False, True)
        RED(cx, ms[:, 8:9], ps, ALU.max)
        TS_(cx, "vector", ms[:, 9:10], ms[:, 8:9], -1.0, None, ALU.mult)
        ACT(cx, ms[:, 0:8], ps, AF.Exp, bias=ms[:, 9:10])
        RED(cx, ms[:, 10:11], ms[:, 0:8], ALU.add)
        OP(cx, "vector", lambda E: E.reciprocal(ms[:, 11:12], ms[:, 10:11]),
           [ms[:, 10:11]], [ms[:, 11:12]])
        RED(cx, ms[:, 12:13], ms[:, 0:8], ALU.max)
        TS_(cx, "vector", ms[:, 16:24], ms[:, 0:8], ms[:, 12:13], None, ALU.is_ge)
        STT(cx, "vector", ms[:, 24:32], ms[:, 16:24], -1e9, ms[:, 0:8],
            ALU.mult, ALU.add)
        RED(cx, ms[:, 13:14], ms[:, 24:32], ALU.max)
        TT_(cx, "vector", ms[:, 14:15], ms[:, 12:13], ms[:, 13:14], ALU.add)
        vs = cx.vs_g.ap()[:, s:s + 1]
        TT_(cx, "vector", vs, ms[:, 14:15], ms[:, 11:12], ALU.mult)
        cnt = cx.count_g.ap()[:, s * NE:(s + 1) * NE]
        TS_(cx, "vector", cnt, ms[:, 0:8], ms[:, 13:14], None, ALU.is_ge)
    for p in range(4):
        for c in range(NCH):
            l = cx.w_apad.ap()[:, c * 512 + 128 * p: c * 512 + 128 * (p + 1)]
            for n in range(2):
                r = cx.xn_sb.ap()[:, c * G + n * 512: c * G + (n + 1) * 512]
                MM(cx, cx.ps_b.ap()[:, n * 512:(n + 1) * 512], l, r,
                   c == 0, c == NCH - 1)
        CP(cx, "vector", cx.hg.ap()[:, p * G:(p + 1) * G], cx.ps_b.ap()[:, :])
    if cx.tap == "vs":
        dv = cx.dr["dbg"].ap().rearrange("(a p) d -> a p d", p=128)
        DMA(cx, "gpsimd", dv[t0 // 128, :, 0:NSUB], cx.vs_g.ap()[:, :], "tapv")
        DMA(cx, "gpsimd", dv[t0 // 128, :, 16:16 + NSUB * NE],
            cx.count_g.ap()[:, :], "tapc")
        cx.S.join("tapvs")


def emit_l0ffn(cx, t0):
    for q in range(NQ):
        for c in range(NCH):
            DMA(cx, "sync", cx.w_wiTq.ap()[:, c * FQ:(c + 1) * FQ],
                cx.sc["b_wiT"].ap()[c * 128:(c + 1) * 128, q * FQ:(q + 1) * FQ])
        for p in range(4):
            DMA(cx, "sync", cx.w_bpadq.ap()[:, p * FQ:(p + 1) * FQ],
                cx.sc["b_bpad"].ap()[p, :, q * FQ:(q + 1) * FQ])
        for c in range(NCH):
            DMA(cx, "sync", cx.w_woTq.ap()[:, c * D:(c + 1) * D],
                cx.sc["b_woT"].ap()[q * FQ + c * 128: q * FQ + (c + 1) * 128, :])

        def xn_of(s):
            return [cx.xn_sb.ap()[:, c * G + s * 128: c * G + (s + 1) * 128]
                    for c in range(NCH)]

        def base_mm(s):
            xn_s = xn_of(s)
            off = 1024 * (s % 2)
            for c in range(NCH):
                for n in range(2):
                    w = cx.w_wiTq.ap()[:, c * FQ + n * 512: c * FQ + (n + 1) * 512]
                    MM(cx, cx.ps_a.ap()[:, off + n * 512: off + (n + 1) * 512],
                       xn_s[c], w, c == 0, False)
            for n in range(2):
                br = cx.r_bi.ap()[0:1, q * FQ + n * 512: q * FQ + (n + 1) * 512]
                MM(cx, cx.ps_a.ap()[:, off + n * 512: off + (n + 1) * 512],
                   cx.ones_rb.ap()[:, :], br, False, True)

        def base_evac(s):
            off = 1024 * (s % 2)
            bb = [cx.base_b, cx.base_b2][s % 2]
            CP(cx, "vector", bb.ap()[:, :], cx.ps_a.ap()[:, off:off + 1024])

        def expert_mms(s, e):
            par = s % 2
            off = 1024 * (e % 2)
            bb = [cx.base_b, cx.base_b2][par]
            p, el = e // 2, e % 2
            hl = cx.hg.ap()[64 * el:64 * el + 64,
                            p * G + s * 128: p * G + (s + 1) * 128]
            dmm = []
            bmm = []
            for n in range(2):
                ps = cx.ps_a.ap()[:, off + n * 512: off + (n + 1) * 512]
                r = cx.w_bpadq.ap()[64 * el:64 * el + 64,
                                    p * FQ + n * 512: p * FQ + (n + 1) * 512]
                dmm.append((ps, hl, r))
                bmm.append((ps, bb.ap()[:, n * 512:(n + 1) * 512]))
            return dmm, bmm

        def expert_tail(s, e):
            par = s % 2
            off = 1024 * (e % 2)
            accp = [(cx.acc0_b, cx.acc1_b), (cx.acc2_b, cx.acc3_b)][par]
            relu = [cx.relu_b, cx.relu_b2][par]
            cntc = cx.count_g.ap()[:, s * NE + e: s * NE + e + 1]
            first = not state_first[par]
            outw = accp[0] if first else relu
            ACT(cx, outw.ap()[:, :], cx.ps_a.ap()[:, off:off + 1024], AF.Relu,
                scale=cntc)
            if not first:
                k = state_cnt[par]
                prev = accp[(k - 1) % 2] if k > 1 else accp[0]
                cur = accp[k % 2]
                TT_(cx, "vector" if par == 0 else "gpsimd", cur.ap()[:, :],
                    prev.ap()[:, :], relu.ap()[:, :], ALU.add)
                state_cnt[par] += 1
            else:
                state_first[par] = True
                state_cnt[par] = 1

        def phaseB(s):
            accfin = [(cx.acc0_b, cx.acc1_b), (cx.acc2_b, cx.acc3_b)][s % 2][(NE - 1) % 2]
            for c in range(NCH):
                TRN(cx, cx.ps_t.ap()[:, c * 128:(c + 1) * 128],
                    accfin.ap()[:, c * 128:(c + 1) * 128], cx.ident_b.ap()[:, :])
            ACT(cx, cx.accfm_b.ap()[:, :], cx.ps_t.ap()[:, :], AF.Copy)
            for c in range(NCH):
                l = cx.accfm_b.ap()[:, c * 128:(c + 1) * 128]
                for n in range(2):
                    r = cx.w_woTq.ap()[:, c * D + n * 512: c * D + (n + 1) * 512]
                    MM(cx, cx.ps_b.ap()[:, n * 512:(n + 1) * 512], l, r,
                       c == 0, (c == NCH - 1) and not (q == 0))
            if q == 0:
                for n in range(2):
                    br = cx.r_bo.ap()[0:1, n * 512:(n + 1) * 512]
                    MM(cx, cx.ps_b.ap()[:, n * 512:(n + 1) * 512],
                       cx.two_rb.ap()[:, :], br, False, True)
            xs = cx.x_sb.ap()[:, s * D:(s + 1) * D]
            vs = cx.vs_g.ap()[:, s:s + 1]
            STT(cx, "vector", xs, cx.ps_b.ap()[:, :], vs, xs, ALU.mult, ALU.add)

        for k in range(NSUB // 2):
            s0, s1 = 2 * k, 2 * k + 1
            base_mm(s0)
            base_evac(s0)
            base_mm(s1)
            base_evac(s1)
            state_first = [False, False]
            state_cnt = [0, 0]
            for e in range(NE):
                e0, e1 = e, e ^ 1
                dm0, bm0 = expert_mms(s0, e0)
                dm1, bm1 = expert_mms(s1, e1)
                for n in range(2):
                    ps, hl, r = dm0[n]
                    MM(cx, ps, hl, r, True, False)
                    ps, hl, r = dm1[n]
                    MM(cx, ps, hl, r, True, False)
                for n in range(2):
                    ps, rb = bm0[n]
                    MM(cx, ps, cx.ident_b.ap()[:, :], rb, False, True)
                    ps, rb = bm1[n]
                    MM(cx, ps, cx.ident_b.ap()[:, :], rb, False, True)
                expert_tail(s0, e0)
                expert_tail(s1, e1)
            phaseB(s0)
            phaseB(s1)
        cx.S.join(f"ffnq{q}")


def emit_l1attn(cx, si, gh, t0):
    gid = si * 2 + gh
    for c in range(NCH):
        DMA(cx, "sync", cx.w_wog.ap()[:, c * D:(c + 1) * D],
            cx.sc["b_wog"].ap()[c * 128:(c + 1) * 128, :])
    for half in range(2):
        cx.S.scope = f"l1proj_g{gid}h{half}"
        emit_l1_proj(cx, half)
        cx.S.join(f"l1p{half}")
        cx.S.scope = f"l1scan_g{gid}h{half}"
        emit_l1_scan(cx, si, half)
        cx.S.join(f"l1s{half}")
    if cx.tap in ("qh", "kh", "kb", "vb"):
        tap_fm_bf16(cx, t0, getattr(cx, cx.tap), 512)
        cx.S.join("tapl1")


def emit_l1_proj(cx, half):
    ht = half * 512
    for h in range(4):
        ps = cx.ps_c.ap()[0:1, 0:512]
        for c in range(NCH):
            l = cx.w_wb.ap()[:, 4 * c + h: 4 * c + h + 1]
            r = cx.xn_sb.ap()[:, c * G + ht: c * G + ht + 512]
            MM(cx, ps, l, r, c == 0, c == NCH - 1)
        ACT(cx, cx.r_tmpf.ap()[:, :], ps, AF.Tanh, scale=0.5)
        dst = cx.r_beta.ap()[0:1, h * 512:(h + 1) * 512]
        TS_(cx, "vector", dst, cx.r_tmpf.ap()[:, :], 0.5, 0.5, ALU.mult, ALU.add)
    for ti, wname in [(0, "b_Wq"), (1, "b_Wk"), (2, "b_Wv")]:
        wp = cx.w_prj2[ti % 2]
        for c in range(NCH):
            DMA(cx, "sync", wp.ap()[:, c * D:(c + 1) * D],
                cx.sc[wname].ap()[c * 128:(c + 1) * 128, :],
                qtag=["qsync", "qsync2"][ti % 2])
        for st4 in range(2):
            for dc in range(NCH):
                r = cx.xn_sb.ap()[:, dc * G + ht: dc * G + ht + 512]
                for k4 in range(4):
                    fc = st4 * 4 + k4
                    l = wp.ap()[:, dc * D + fc * 128: dc * D + (fc + 1) * 128]
                    MM(cx, cx.ps_a.ap()[:, k4 * 512:(k4 + 1) * 512], l, r,
                       dc == 0, dc == NCH - 1)
            for k4 in range(4):
                fc = st4 * 4 + k4
                emit_conv_silu(cx, ti, fc, cx.ps_a.ap()[:, k4 * 512:(k4 + 1) * 512])
        emit_l1_postproj(cx, ti, half)


def emit_conv_silu(cx, ti, fc, ps):
    tcol = (ti * NCH + fc) * 3
    tail = cx.tails.ap()[:, tcol:tcol + 3]
    w3 = cx.convw.ap()[:, ti * 32 + fc * 4 + 3: ti * 32 + fc * 4 + 4]
    w2 = cx.convw.ap()[:, ti * 32 + fc * 4 + 2: ti * 32 + fc * 4 + 3]
    w1 = cx.convw.ap()[:, ti * 32 + fc * 4 + 1: ti * 32 + fc * 4 + 2]
    w0 = cx.convw.ap()[:, ti * 32 + fc * 4 + 0: ti * 32 + fc * 4 + 1]
    aug = cx.aug_f.ap()
    CP(cx, "vector", aug[:, 0:3], tail)
    CP(cx, "vector", aug[:, 3:515], ps)
    CP(cx, "vector", tail, aug[:, 512:515])
    TS_(cx, "vector", cx.csA_f.ap()[:, :], ps, w3, None, ALU.mult)
    STT(cx, "vector", cx.csB_f.ap()[:, :], aug[:, 2:514], w2,
        cx.csA_f.ap()[:, :], ALU.mult, ALU.add)
    STT(cx, "vector", cx.csA_f.ap()[:, :], aug[:, 1:513], w1,
        cx.csB_f.ap()[:, :], ALU.mult, ALU.add)
    STT(cx, "vector", cx.csB_f.ap()[:, :], aug[:, 0:512], w0,
        cx.csA_f.ap()[:, :], ALU.mult, ALU.add)
    dst = cx.silu_f.ap()[:, fc * 512:(fc + 1) * 512]
    ACT(cx, dst, cx.csB_f.ap()[:, :], AF.Silu)


def emit_l1_postproj(cx, ti, half):
    sil = cx.silu_f.ap()
    if ti in (0, 1):
        for h in range(4):
            psr = cx.ps_c.ap()[0:1, 0:512]
            for j in range(2):
                c = 2 * h + j
                ACT(cx, cx.sq_b.ap()[:, :], sil[:, c * 512:(c + 1) * 512],
                    AF.Square)
                MM(cx, psr, cx.ones_cb.ap()[:, :], cx.sq_b.ap()[:, :],
                   j == 0, j == 1)
            ACT(cx, cx.r_tmpf.ap()[:, :], psr, AF.Ln)
            dst = cx.r_rq.ap()[0:1, h * 512:(h + 1) * 512]
            ACT(cx, dst, cx.r_tmpf.ap()[:, :], AF.Exp, scale=-0.5)
        if ti == 1:
            for h in range(4):
                TT_(cx, "vector", cx.r_rkb.ap()[0:1, h * 512:(h + 1) * 512],
                    cx.r_rq.ap()[0:1, h * 512:(h + 1) * 512],
                    cx.r_beta.ap()[0:1, h * 512:(h + 1) * 512], ALU.mult)
    outs = [(cx.qh, cx.r_rq), (cx.kh, cx.r_rq), (cx.vb, cx.r_beta)][ti]
    out_sb, rows = outs
    for h in range(4):
        psb = cx.ps_b.ap()[:, 0:512]
        MM(cx, psb, cx.ones_rb.ap()[:, :], rows.ap()[0:1, h * 512:(h + 1) * 512],
           True, True)
        for j in range(2):
            c = 2 * h + j
            TT_(cx, "vector", out_sb.ap()[:, c * 512:(c + 1) * 512],
                sil[:, c * 512:(c + 1) * 512], psb, ALU.mult)
        if ti == 1:
            psb2 = cx.ps_b.ap()[:, 512:1024]
            MM(cx, psb2, cx.ones_rb.ap()[:, :],
               cx.r_rkb.ap()[0:1, h * 512:(h + 1) * 512], True, True)
            for j in range(2):
                c = 2 * h + j
                TT_(cx, "vector", cx.kb.ap()[:, c * 512:(c + 1) * 512],
                    sil[:, c * 512:(c + 1) * 512], psb2, ALU.mult)


def emit_l1_scan(cx, si, half):
    msk = cx.masksl.ap()[:, :]
    id64f = cx.ident_f.ap()[0:C, 0:C]

    def bulk(p):
        sg = p % 2
        tmq, tmk, tmkb, tmvb = cx.tm_set[sg]
        for fm, tm in [(cx.qh, tmq), (cx.kh, tmk), (cx.kb, tmkb), (cx.vb, tmvb)]:
            for c in range(NCH):
                TRN(cx, cx.ps_t.ap()[:, c * 128:(c + 1) * 128],
                    fm.ap()[:, c * 512 + p * 128: c * 512 + (p + 1) * 128],
                    cx.ident_b.ap()[:, :])
            CP(cx, "vector", tm.ap()[:, :], cx.ps_t.ap()[:, :])
        for ci in range(2):
            b0 = C * ci
            psA = cx.ps_c.ap()[0:C, 0:C]
            for c in range(NCH):
                l = cx.kb.ap()[:, c * 512 + p * 128 + b0: c * 512 + p * 128 + b0 + C]
                r = cx.kh.ap()[:, c * 512 + p * 128 + b0: c * 512 + p * 128 + b0 + C]
                MM(cx, psA, l, r, c == 0, c == NCH - 1)
            TT_(cx, "vector", cx.Lm.ap()[:, :], psA, msk, ALU.mult)
            ps1 = cx.ps_a.ap()[0:C, 0:C]
            TRN(cx, ps1, cx.Lm.ap()[:, :], id64f)
            CP(cx, "vector", cx.LmT.ap()[:, :], ps1)
            STT(cx, "vector", cx.G_a.ap()[:, :], cx.LmT.ap()[:, :], -1.0,
                id64f, ALU.mult, ALU.add)
            X, XT = cx.Lm, cx.LmT
            Xn = [cx.X_a, cx.X_b]
            XTn = [cx.XT_a, cx.XT_b]
            Gs = [cx.G_a, cx.G_b, cx.G_c]
            gi = 0
            for lvl in range(5):
                ps_sq = cx.ps_c.ap()[0:C, 64:128]
                ps_sq2 = cx.ps_a.ap()[0:C, 512:576]
                ps_pr = cx.ps_a.ap()[0:C, 1024:1088]
                MM(cx, ps_sq, XT.ap()[0:C, :], X.ap()[0:C, :], True, True)
                CP(cx, "vector", Xn[lvl % 2].ap()[:, :], ps_sq)
                X2 = Xn[lvl % 2]
                if lvl < 4:
                    MM(cx, ps_sq2, X.ap()[0:C, :], XT.ap()[0:C, :], True, True)
                    CP(cx, "vector", XTn[lvl % 2].ap()[:, :], ps_sq2)
                G = Gs[gi % 3]
                Gnew = Gs[(gi + 1) % 3]
                MM(cx, ps_pr, X2.ap()[0:C, :], G.ap()[0:C, :], True, True)
                TT_(cx, "vector", Gnew.ap()[:, :], G.ap()[:, :], ps_pr, ALU.add)
                gi += 1
                X, XT = Xn[lvl % 2], XTn[lvl % 2]
            Gfin = Gs[gi % 3]
            TTp = cx.TT4[p]
            if ci == 0:
                STT(cx, "vector", TTp.ap()[0:C, :], Gfin.ap()[:, :], 3.0,
                    cx.twoI.ap()[0:C, :], ALU.mult, ALU.subtract)
            else:
                psg = cx.ps_a.ap()[64:128, 1536:1600]
                MM(cx, psg, id64f, Gfin.ap()[0:C, :], True, True, tp=(0, 64))
                STT(cx, "vector", TTp.ap()[64:128, :], psg, 3.0,
                    cx.twoI.ap()[64:128, :], ALU.mult, ALU.subtract)
            TTl = TTp.ap()[b0:b0 + C, :]
            Wb_, Ub_ = cx.WU_set[sg]
            for n in range(2):
                MM(cx, cx.ps_a.ap()[b0:b0 + C, n * 512:(n + 1) * 512], TTl,
                   tmkb.ap()[b0:b0 + C, n * 512:(n + 1) * 512], True, True,
                   tp=(None if ci == 0 else (64, 64)))
                MM(cx, cx.ps_a.ap()[b0:b0 + C, 1024 + n * 512:1024 + (n + 1) * 512],
                   TTl, tmvb.ap()[b0:b0 + C, n * 512:(n + 1) * 512], True, True,
                   tp=(None if ci == 0 else (64, 64)))
            ACT(cx, Wb_.ap()[b0:b0 + C, :], cx.ps_a.ap()[b0:b0 + C, 0:1024], AF.Copy)
            ACT(cx, Ub_.ap()[b0:b0 + C, :], cx.ps_a.ap()[b0:b0 + C, 1024:2048],
                AF.Copy)

    def seq(p):
        sg = p % 2
        tmq, tmk, tmkb, tmvb = cx.tm_set[sg]
        Wb_, Ub_ = cx.WU_set[sg]
        otm = cx.o_set[sg]
        for ci in range(2):
            b0 = C * ci
            tp = None if ci == 0 else (0, 64)
            CP(cx, "vector", cx.r_sdb.ap()[0:1, :], cx.r_sd.ap()[0:1, :])
            for n in range(2):
                MM(cx, cx.ps_b.ap()[b0:b0 + C, n * 512:(n + 1) * 512],
                   cx.ones_rb.ap()[0:1, 0:C],
                   cx.r_sdb.ap()[0:1, n * 512:(n + 1) * 512], True, True, tp=tp)
            sdb = cx.ps_b.ap()[b0:b0 + C, 0:1024]
            TT_(cx, "vector", cx.ou_f.ap()[b0:b0 + C, :],
                Wb_.ap()[b0:b0 + C, :], sdb, ALU.mult)
            TT_(cx, "vector", cx.u_f.ap()[b0:b0 + C, :],
                Ub_.ap()[b0:b0 + C, :], cx.ou_f.ap()[b0:b0 + C, :], ALU.subtract)
            TT_(cx, "vector", cx.oi_f.ap()[b0:b0 + C, :],
                tmq.ap()[b0:b0 + C, :], sdb, ALU.mult)
            TT_(cx, "vector", cx.t3_f.ap()[b0:b0 + C, :],
                tmq.ap()[b0:b0 + C, :], tmk.ap()[b0:b0 + C, :], ALU.mult)
            RED(cx, cx.aq_c.ap()[b0:b0 + C, 0:1], cx.t3_f.ap()[b0:b0 + C, :],
                ALU.add)
            TT_(cx, "vector", cx.aq_c.ap()[b0:b0 + C, 1:2],
                cx.aq_c.ap()[b0:b0 + C, 0:1],
                cx.aimask.ap()[b0:b0 + C, si:si + 1], ALU.mult)
            STT(cx, "vector", otm.ap()[b0:b0 + C, :], cx.u_f.ap()[b0:b0 + C, :],
                cx.aq_c.ap()[b0:b0 + C, 1:2], cx.oi_f.ap()[b0:b0 + C, :],
                ALU.mult, ALU.add)
            TT_(cx, "vector", cx.t3_b.ap()[b0:b0 + C, :],
                cx.tm_set[sg][1].ap()[b0:b0 + C, :], cx.u_f.ap()[b0:b0 + C, :],
                ALU.mult)
            for n in range(2):
                psc = cx.ps_b.ap()[0:1, n * 512:(n + 1) * 512]
                MM(cx, psc, cx.ones_cb.ap()[b0:b0 + C, 0:1],
                   cx.t3_b.ap()[b0:b0 + C, n * 512:(n + 1) * 512], True, True)
            sdr = cx.r_sd.ap()[0:1, 0:1024]
            TT_(cx, "vector", sdr, sdr, cx.ps_b.ap()[0:1, 0:1024], ALU.add)

    def ophase(p):
        sg = p % 2
        otm = cx.o_set[sg]
        ACT(cx, cx.osq_f.ap()[:, :], otm.ap()[:, :], AF.Square)
        for h in range(4):
            RED(cx, cx.rms_c.ap()[:, h:h + 1],
                cx.osq_f.ap()[:, h * 256:(h + 1) * 256], ALU.add)
        ACT(cx, cx.rms_c.ap()[:, 4:8], cx.rms_c.ap()[:, 0:4], AF.Ln,
            scale=1.0 / 256, bias=cx.cvals.ap()[:, 0:1])
        ACT(cx, cx.rms_c.ap()[:, 8:12], cx.rms_c.ap()[:, 4:8], AF.Exp, scale=-0.5)
        for h in range(4):
            TS_(cx, "vector", cx.on_b.ap()[:, h * 256:(h + 1) * 256],
                otm.ap()[:, h * 256:(h + 1) * 256],
                cx.rms_c.ap()[:, 8 + h:9 + h], None, ALU.mult)
        for c in range(NCH):
            TRN(cx, cx.ps_t.ap()[:, c * 128:(c + 1) * 128],
                cx.on_b.ap()[:, c * 128:(c + 1) * 128], cx.ident_b.ap()[:, :])
        CP(cx, "vector", cx.ofm_b.ap()[:, :], cx.ps_t.ap()[:, :])
        for c in range(NCH):
            l = cx.ofm_b.ap()[:, c * 128:(c + 1) * 128]
            for n in range(2):
                r = cx.w_wog.ap()[:, c * D + n * 512: c * D + (n + 1) * 512]
                MM(cx, cx.ps_b.ap()[:, n * 512:(n + 1) * 512], l, r,
                   c == 0, c == NCH - 1)
        sidx = half * 4 + p
        xs = cx.x_sb.ap()[:, sidx * D:(sidx + 1) * D]
        TT_(cx, "vector", xs, xs, cx.ps_b.ap()[:, :], ALU.add)

    bulk(0)
    bulk(1)
    seq(0)
    bulk(2)
    ophase(0)
    seq(1)
    bulk(3)
    ophase(1)
    seq(2)
    ophase(2)
    seq(3)
    ophase(3)


def emit_l1ffn(cx, t0):
    for c in range(NCH):
        DMA(cx, "sync", cx.w_u1.ap()[:, c * 256:(c + 1) * 256],
            cx.sc["b_u1"].ap()[c * 128:(c + 1) * 128, :])
    for rc in range(2):
        DMA(cx, "sync", cx.w_v1.ap()[:, rc * DFF:(rc + 1) * DFF],
            cx.sc["b_v1"].ap()[rc * 128:(rc + 1) * 128, :])
        DMA(cx, "sync", cx.w_v2.ap()[:, rc * D:(rc + 1) * D],
            cx.sc["b_v2"].ap()[rc * 128:(rc + 1) * 128, :])
    for fc in range(32):
        DMA(cx, "sync", cx.w_u2.ap()[:, fc * 256:(fc + 1) * 256],
            cx.sc["b_u2"].ap()[fc * 128:(fc + 1) * 128, :])
    for rc in range(2):
        for n in range(2):
            ps = cx.ps_b.ap()[:, n * 512:(n + 1) * 512]
            for dc in range(NCH):
                l = cx.w_u1.ap()[:, dc * 256 + rc * 128: dc * 256 + (rc + 1) * 128]
                r = cx.xn_sb.ap()[:, dc * G + n * 512: dc * G + (n + 1) * 512]
                MM(cx, ps, l, r, dc == 0, dc == NCH - 1)
            CP(cx, "vector", cx.P_b.ap()[:, rc * G + n * 512: rc * G + (n + 1) * 512], ps)

    def v2_sub(s):
        for n in range(2):
            ps = cx.ps_b.ap()[:, n * 512:(n + 1) * 512]
            for rc in range(2):
                l = cx.Q_b.ap()[:, rc * G + s * 128: rc * G + (s + 1) * 128]
                r = cx.w_v2.ap()[:, rc * D + n * 512: rc * D + (n + 1) * 512]
                MM(cx, ps, l, r, rc == 0, False)
            br = cx.r_b2.ap()[0:1, n * 512:(n + 1) * 512]
            MM(cx, ps, cx.ones_rb.ap()[:, :], br, False, True)
        xs = cx.x_sb.ap()[:, s * D:(s + 1) * D]
        TT_(cx, "vector", xs, xs, cx.ps_b.ap()[:, :], ALU.add)

    for th in range(2):
        t0h = th * 512
        for fc in range(32):
            slot = 1024 + 512 * (fc % 2)
            ps_h = cx.ps_a.ap()[:, slot:slot + 512]
            for rc in range(2):
                l = cx.w_v1.ap()[:, rc * DFF + fc * 128: rc * DFF + (fc + 1) * 128]
                r = cx.P_b.ap()[:, rc * G + t0h: rc * G + t0h + 512]
                MM(cx, ps_h, l, r, rc == 0, rc == 1)
            hb = cx.H_b.ap()[:, 512 * (fc % 2):512 * (fc % 2) + 512]
            ACT(cx, hb, ps_h, AF.Gelu_apprx_tanh, bias=cx.b1cols.ap()[:, fc:fc + 1])
            for rc in range(2):
                ps = cx.ps_a.ap()[:, rc * 512:(rc + 1) * 512]
                l = cx.w_u2.ap()[:, fc * 256 + rc * 128: fc * 256 + (rc + 1) * 128]
                MM(cx, ps, l, hb, fc == 0, fc == 31)
            if th == 1 and fc % 8 == 7:
                v2_sub(fc // 8)
        CP(cx, "vector", cx.Q_b.ap()[:, t0h:t0h + 512], cx.ps_a.ap()[:, 0:512])
        CP(cx, "vector", cx.Q_b.ap()[:, 1024 + t0h:1024 + t0h + 512],
           cx.ps_a.ap()[:, 512:1024])
    for s in range(4, 8):
        v2_sub(s)



def kernel(**inputs):
    cores = list(range(8))
    in_maps_full = [prep_core_inputs(inputs, c) for c in cores]
    if LAUNCH_MODE == "fused":
        nc = build_program(("l0", "l1"), DEBUG_TAP)
        res = run_bass_kernel_spmd(nc, in_maps_full, core_ids=cores)
        outs = [r["out"] for r in res.results]
    else:
        nc0 = build_program(("l0",), DEBUG_TAP)
        m0 = [{k: v for k, v in m.items() if k not in L0_ONLY_DROP}
              for m in in_maps_full]
        r0 = run_bass_kernel_spmd(nc0, m0, core_ids=cores)
        nc1 = build_program(("l1",), DEBUG_TAP)
        m1 = []
        for i, m in enumerate(in_maps_full):
            mm = {k: v for k, v in m.items() if k not in L1_ONLY_DROP}
            mm["x"] = np.ascontiguousarray(r0.results[i]["out"])
            m1.append(mm)
        r1 = run_bass_kernel_spmd(nc1, m1, core_ids=cores)
        outs = [r["out"] for r in r1.results]
    out = np.concatenate([o.reshape(NSEQ, S, D) for o in outs], axis=0)
    return out.astype(np.float32)


# version 15 (final)
# speedup vs baseline: 1.0571x; 1.0571x over previous
import numpy as np
from contextlib import ExitStack

import concourse.bass as bass
import concourse.mybir as mybir
from concourse.bass_utils import run_bass_kernel_spmd

F32 = mybir.dt.float32
BF16 = mybir.dt.bfloat16
AF = mybir.ActivationFunctionType
ALU = mybir.AluOpType
AX = mybir.AxisListType.X

D = 1024
DFF = 4096
NCH = 8
S = 2048
NSEQ = 4
G = 1024
NSUB = G // 128
C = 64
NE = 8
NQ = 4
FQ = DFF // NQ
TOK = NSEQ * S

LAUNCH_MODE = "fused"
DEBUG_TAP = None
DEBUG_GROUPS = None
INTERLEAVE_FFN = True

ENG = ("tensor", "vector", "scalar", "gpsimd", "sync")
DMAQ = {"sync": "qsync", "gpsimd": "qpool", "scalar": "qscal",
        "vector": "qvec", "tensor": "qtens"}
ALLQ = ["qsync", "qpool", "qscal", "qvec", "qtens", "qsync2"]
_BF16_TAPS = ("xn0", "xn1", "qh", "kh", "kb", "vb")



def _footprint(ap):
    t = ap.tensor
    shape = list(t.shape)
    rowlen = shape[-1]
    off = ap.offset
    p0 = off // rowlen
    c0 = off % rowlen
    pspan = 1
    cspan = 1
    for step, count in ap.ap:
        if count <= 1:
            continue
        if step > 0 and step % rowlen == 0:
            pspan += (step // rowlen) * (count - 1)
        else:
            cspan += step * (count - 1)
    name = t.name
    if "PSUM" in str(t.space):
        bank_elems = 2048 // (2 if ap.dtype == BF16 else 4)
        c0b = (c0 // bank_elems) * bank_elems
        c1b = ((c0 + cspan + bank_elems - 1) // bank_elems) * bank_elems
        return (name, 0, 128, c0b, c1b, True)
    return (name, p0, p0 + pspan, c0, c0 + cspan, False)


def _overlap(a, b):
    return (a[0] == b[0] and a[1] < b[2] and b[1] < a[2]
            and a[3] < b[4] and b[3] < a[4])


class Sched:
    def __init__(self, nc):
        self.nc = nc
        self.ops = []
        self.accesses = {}
        self.scope = "init"

    def op(self, engine, fn, reads=(), writes=(), dma=False, tag="", qtag=None):
        i = len(self.ops)
        rfp = [_footprint(a) for a in reads]
        wfp = [_footprint(a) for a in writes]
        deps = set()
        for fp in rfp:
            kinds = ("w", "r") if fp[5] else ("w",)
            for (oj, kind, fp2) in self.accesses.get(fp[0], ()):
                if kind in kinds and _overlap(fp, fp2):
                    deps.add(oj)
        for fp in wfp:
            for (oj, kind, fp2) in self.accesses.get(fp[0], ()):
                if _overlap(fp, fp2):
                    deps.add(oj)
        q = (qtag or DMAQ[engine]) if dma else None
        self.ops.append(dict(kind="op", i=i, eng=engine, fn=fn, deps=deps,
                             dma=q, inc=False, tag=tag, scope=self.scope))
        for fp in rfp:
            self.accesses.setdefault(fp[0], []).append((i, "r", fp))
        for fp in wfp:
            self.accesses.setdefault(fp[0], []).append((i, "w", fp))
        return i

    def join(self, tag=""):
        self.ops.append(dict(kind="join", i=len(self.ops), tag=tag))
        self.accesses.clear()

    def finalize(self):
        nc = self.nc
        ops = self.ops
        for o in ops:
            if o["kind"] == "op":
                for d in o["deps"]:
                    od = ops[d]
                    if od["dma"]:
                        continue
                    if od["eng"] == "tensor" and o["eng"] == "tensor":
                        continue
                    od["inc"] = True
        last = {}
        for o in ops:
            if o["kind"] == "op":
                if not o["dma"]:
                    last[o["eng"]] = o["i"]
            else:
                for e, oi in last.items():
                    ops[oi]["inc"] = True
                last = {}
        ecnt = {e: 0 for e in ENG}
        qcnt = {q: 0 for q in ALLQ}
        for o in ops:
            if o["kind"] != "op":
                continue
            if o["dma"]:
                qcnt[o["dma"]] += 1
                o["q_done"] = qcnt[o["dma"]] * 16
            if o["inc"]:
                ecnt[o["eng"]] += 1
                o["e_done"] = ecnt[o["eng"]]
            else:
                o["e_done"] = None
            o["e_seen"] = ecnt[o["eng"]]
        clocks = list(ENG) + list(ALLQ)
        guar = {e: {c: 0 for c in clocks} for e in ENG}
        cum_e = {e: 0 for e in ENG}
        cum_q = {q: 0 for q in ALLQ}
        for o in ops:
            if o["kind"] == "join":
                tgt = {**{e: cum_e[e] for e in ENG},
                       **{q: cum_q[q] * 16 for q in cum_q}}
                o["jtgt"] = tgt
                for e in ENG:
                    for c in clocks:
                        if guar[e][c] < tgt[c]:
                            guar[e][c] = tgt[c]
                continue
            e = o["eng"]
            waits = []
            for dd in sorted(o["deps"]):
                od = ops[dd]
                if od["dma"]:
                    c, v = od["dma"], cum_q[od["dma"]] * 16
                else:
                    c, v = od["eng"], od["e_done"]
                    if c == e == "tensor":
                        continue
                    if v is None:
                        continue
                if guar[e][c] < v:
                    waits.append((c, v))
                    guar[e][c] = v
                    for c2, v2 in od["cvec"].items():
                        if guar[e][c2] < v2:
                            guar[e][c2] = v2
            o["waits"] = waits
            cv = dict(guar[e])
            cv[e] = o["e_seen"]
            if o["dma"]:
                cum_q[o["dma"]] += 1
                cv[o["dma"]] = o["q_done"]
            o["cvec"] = cv
            if o["inc"]:
                cum_e[e] += 1
        with ExitStack() as stx:
            sems = {}
            for c in clocks:
                sems[c] = stx.enter_context(nc.semaphore(f"sem_{c}"))
            blk = stx.enter_context(nc.Block())
            for e in ENG:
                self._emit_engine(blk, e, sems)

    def _emit_engine(self, blk, e, sems):
        ops = self.ops
        nc = self.nc

        def body(eng):
            cur = [None]
            cm = [None]

            def setscope(s):
                if s == cur[0]:
                    return
                if cm[0] is not None:
                    cm[0].__exit__(None, None, None)
                    cm[0] = None
                if s is not None:
                    cm[0] = nc.named_scope(s)
                    cm[0].__enter__()
                cur[0] = s
            for o in ops:
                if o["kind"] == "join":
                    for c, v in o["jtgt"].items():
                        if v > 0:
                            eng.wait_ge(sems[c], v)
                    continue
                if o["eng"] != e:
                    continue
                setscope(o.get("scope"))
                for c, v in o["waits"]:
                    eng.wait_ge(sems[c], v)
                inst = o["fn"](eng)
                if o["dma"]:
                    inst = inst.then_inc(sems[o["dma"]], 16)
                if o["inc"]:
                    inst.then_inc(sems[e], 1)
            setscope(None)
        getattr(blk, e)(body)



def prep_core_inputs(inputs, core):
    x = np.ascontiguousarray(
        np.asarray(inputs["L0_attn_x"])[core * NSEQ:(core + 1) * NSEQ]
        .reshape(TOK, D))
    d = {"x": x}
    d["wqkvT"] = np.asarray(inputs["L0_attn_W_qkv"]).T
    d["bqkv"] = np.asarray(inputs["L0_attn_b_qkv"]).reshape(1, 3 * D)
    d["wgT"] = np.asarray(inputs["L0_attn_W_gate"]).T
    d["bg"] = np.asarray(inputs["L0_attn_b_gate"]).reshape(1, D)
    d["wiT"] = np.asarray(inputs["L0_ffn_wi"]).T
    d["bi"] = np.asarray(inputs["L0_ffn_bi"]).reshape(1, DFF)
    d["woT"] = np.asarray(inputs["L0_ffn_wo"]).T
    d["bo"] = np.asarray(inputs["L0_ffn_bo"]).reshape(1, D)
    d["lAT"] = np.asarray(inputs["L0_ffn_lora_A"]).transpose(0, 2, 1)
    d["lBT"] = np.asarray(inputs["L0_ffn_lora_B"]).transpose(0, 2, 1)
    d["gwT"] = np.asarray(inputs["L0_ffn_gate_w"]).T
    d["gb"] = np.asarray(inputs["L0_ffn_gate_b"]).reshape(1, NE)
    d["Wq"] = np.asarray(inputs["L1_attn_Wq"])
    d["Wk"] = np.asarray(inputs["L1_attn_Wk"])
    d["Wv"] = np.asarray(inputs["L1_attn_Wv"])
    d["Wb"] = np.asarray(inputs["L1_attn_Wb"])
    d["convw"] = np.stack([np.asarray(inputs["L1_attn_conv_q"])[:, 0, :],
                           np.asarray(inputs["L1_attn_conv_k"])[:, 0, :],
                           np.asarray(inputs["L1_attn_conv_v"])[:, 0, :]], axis=0)
    d["gcols"] = np.asarray(inputs["L1_attn_g"]).reshape(2, 128).T
    d["Wo"] = np.asarray(inputs["L1_attn_Wo"])
    d["u1"] = np.asarray(inputs["L1_ffn_U1"])
    d["v1"] = np.asarray(inputs["L1_ffn_V1"])
    d["b1cols"] = np.asarray(inputs["L1_ffn_b1"]).reshape(32, 128).T
    d["u2"] = np.asarray(inputs["L1_ffn_U2"])
    d["v2"] = np.asarray(inputs["L1_ffn_V2"])
    d["b2"] = np.asarray(inputs["L1_ffn_b2"]).reshape(1, D)
    m = np.zeros((C, NSEQ), dtype=np.float32)
    for si in range(NSEQ):
        m[:, si] = (np.arange(C) <= core * NSEQ + si).astype(np.float32)
    d["aimask"] = np.tile(m, (2, 1))
    return {k: np.ascontiguousarray(np.asarray(v, dtype=np.float32))
            for k, v in d.items()}


L0_ONLY_DROP = {"Wq", "Wk", "Wv", "Wb", "convw", "gcols", "Wo", "u1", "v1",
                "b1cols", "u2", "v2", "b2", "aimask"}
L1_ONLY_DROP = {"wqkvT", "bqkv", "wgT", "bg", "wiT", "bi", "woT", "bo",
                "lAT", "lBT", "gwT", "gb"}



class Ctx:
    pass


def OP(cx, eng, fn, reads=(), writes=(), tag=""):
    return cx.S.op(eng, fn, reads, writes, False, tag)


def DMA(cx, eng, out, in_, tag="", qtag=None):
    rd = [in_] if "DRAM" not in str(in_.tensor.space) else []
    wr = [out] if "DRAM" not in str(out.tensor.space) else []
    return cx.S.op(eng, lambda E, o=out, i=in_: E.dma_start(out=o, in_=i),
                   rd, wr, True, tag, qtag)


def MM(cx, o, l, r, start, stop, tp=None):
    if tp is None:
        OP(cx, "tensor", lambda E, o=o, l=l, r=r, s=start, p=stop:
            E.matmul(o, l, r, start=s, stop=p), [l, r], [o])
    else:
        OP(cx, "tensor", lambda E, o=o, l=l, r=r, s=start, p=stop, t=tp:
            E.matmul(o, l, r, start=s, stop=p, tile_position=t), [l, r], [o])


def TT_(cx, eng, o, a, b, op):
    OP(cx, eng, lambda E, o=o, a=a, b=b, op=op: E.tensor_tensor(o, a, b, op=op),
       [a, b], [o])


def TS_(cx, eng, o, a, s1, s2, op0, op1=None):
    rd = [a] + [s for s in (s1, s2) if hasattr(s, "tensor")]
    if op1 is None:
        OP(cx, eng, lambda E, o=o, a=a, s1=s1, s2=s2, op0=op0:
            E.tensor_scalar(o, a, s1, s2, op0=op0), rd, [o])
    else:
        OP(cx, eng, lambda E, o=o, a=a, s1=s1, s2=s2, op0=op0, op1=op1:
            E.tensor_scalar(o, a, s1, s2, op0=op0, op1=op1), rd, [o])


def STT(cx, eng, o, a, s, b, op0, op1):
    rd = [a, b] + ([s] if hasattr(s, "tensor") else [])
    OP(cx, eng, lambda E, o=o, a=a, s=s, b=b, op0=op0, op1=op1:
        E.scalar_tensor_tensor(o, a, s, b, op0=op0, op1=op1), rd, [o])


def CP(cx, eng, o, a):
    OP(cx, eng, lambda E, o=o, a=a: E.tensor_copy(o, a), [a], [o])


def ACT(cx, o, a, func, bias=0.0, scale=1.0, accum=None):
    rd = [a] + [s for s in (bias, scale) if hasattr(s, "tensor")]
    wr = [o] + ([accum] if accum is not None else [])
    if accum is None:
        OP(cx, "scalar", lambda E, o=o, a=a, f=func, b=bias, s=scale:
            E.activation(o, a, f, bias=b, scale=s), rd, wr)
    else:
        OP(cx, "scalar", lambda E, o=o, a=a, f=func, b=bias, s=scale, ac=accum:
            E.activation(o, a, f, bias=b, scale=s, accum_out=ac), rd, wr)


def TRN(cx, o, a, ident):
    OP(cx, "tensor", lambda E, o=o, a=a, i=ident: E.transpose(o, a, i),
       [a, ident], [o])


def RED(cx, o, a, op):
    OP(cx, "vector", lambda E, o=o, a=a, op=op: E.tensor_reduce(o, a, axis=AX, op=op),
       [a], [o])



def build_program(stages=("l0", "l1"), debug_tap=None):
    nc = bass.Bass("TRN2", target_bir_lowering=False)
    cx = Ctx()
    cx.nc = nc
    cx.S = Sched(nc)
    cx.tap = debug_tap
    import ml_dtypes

    dr = {}
    dr["x"] = nc.dram_tensor("x", [TOK, D], F32, kind="ExternalInput")
    inspecs = [("wqkvT", [D, 3 * D]), ("bqkv", [1, 3 * D]), ("wgT", [D, D]),
               ("bg", [1, D]), ("wiT", [D, DFF]), ("bi", [1, DFF]),
               ("woT", [DFF, D]), ("bo", [1, D]), ("lAT", [NE, D, 16]),
               ("lBT", [NE, 16, DFF]), ("gwT", [D, NE]), ("gb", [1, NE]),
               ("Wq", [D, D]), ("Wk", [D, D]), ("Wv", [D, D]), ("Wb", [D, 4]),
               ("convw", [3, D, 4]), ("gcols", [128, 2]), ("Wo", [D, D]),
               ("u1", [D, 256]), ("v1", [256, DFF]), ("b1cols", [128, 32]),
               ("u2", [DFF, 256]), ("v2", [256, D]), ("b2", [1, D]),
               ("aimask", [128, NSEQ])]
    drop = set()
    if stages == ("l0",):
        drop = L0_ONLY_DROP
    elif stages == ("l1",):
        drop = L1_ONLY_DROP
    for n, shp in inspecs:
        if n not in drop:
            dr[n] = nc.dram_tensor(n, shp, F32, kind="ExternalInput")
    dr["out"] = nc.dram_tensor("out", [TOK, D], F32, kind="ExternalOutput")
    if debug_tap:
        dt = BF16 if debug_tap in _BF16_TAPS else F32
        dr["dbg"] = nc.dram_tensor("dbg", [TOK, DFF], dt, kind="ExternalOutput")
    cx.dr = dr

    sc = {}
    if "l0" in stages:
        for n, shp in [("b_wqkvT", [D, 3 * D]), ("b_wgT", [D, D]),
                       ("b_wiT", [D, DFF]), ("b_woT", [DFF, D]),
                       ("b_gwT", [D, NE]), ("b_apad", [D, 512]),
                       ("b_bpad", [4, 128, DFF])]:
            sc[n] = nc.dram_tensor(n, shp, BF16, kind="Internal")
    if "l1" in stages:
        for n, shp in [("b_Wq", [D, D]), ("b_Wk", [D, D]), ("b_Wv", [D, D]),
                       ("b_wog", [D, D]), ("b_u1", [D, 256]),
                       ("b_v1", [256, DFF]), ("b_u2", [DFF, 256]),
                       ("b_v2", [256, D])]:
            sc[n] = nc.dram_tensor(n, shp, BF16, kind="Internal")
    cx.sc = sc

    inl = {}
    inl["ident_f"] = nc.inline_tensor(np.eye(128, dtype=np.float32), name="i_idf")
    inl["ident_b"] = nc.inline_tensor(np.eye(128, dtype=ml_dtypes.bfloat16),
                                      name="i_idb")
    inl["utones"] = nc.inline_tensor(np.triu(np.ones((128, 128), np.float32)),
                                     name="i_ut")
    inl["masksl"] = nc.inline_tensor(
        np.tril(np.ones((C, C), np.float32), -1), name="i_msl")
    inl["twoI"] = nc.inline_tensor(
        np.tile(2.0 * np.eye(C), (2, 1)).astype(np.float32), name="i_2I")
    inl["ones_r"] = nc.inline_tensor(np.ones((1, 128), np.float32), name="i_1r")
    inl["ones_rb"] = nc.inline_tensor(np.ones((1, 128), ml_dtypes.bfloat16),
                                      name="i_1rb")
    inl["two_rb"] = nc.inline_tensor(np.full((1, 128), 2.0, ml_dtypes.bfloat16),
                                     name="i_2rb")
    inl["ones_c"] = nc.inline_tensor(np.ones((128, 1), np.float32), name="i_1c")
    inl["ones_cb"] = nc.inline_tensor(np.ones((128, 1), ml_dtypes.bfloat16),
                                      name="i_1cb")
    inl["utones_b"] = nc.inline_tensor(
        np.triu(np.ones((128, 128))).astype(ml_dtypes.bfloat16), name="i_utb")
    cx.inl = inl

    ARENA = 208448
    arena = nc.alloc_sbuf_tensor("my_arena", [128, ARENA], mybir.dt.uint8)
    base = [None]
    off = [None]

    def alloc(name, shape, dtype, align=64):
        if off[0] is None:
            mloc = nc.lookup_mloc(arena)
            base[0] = mloc.addr
            off[0] = base[0]
        off[0] = (off[0] + align - 1) // align * align
        t = nc.alloc_sbuf_tensor_at(name, list(shape), dtype, offset=off[0])
        off[0] += shape[-1] * (2 if dtype == BF16 else 4)
        return t

    cx.x_sb = alloc("x_sb", [128, NSUB * D], F32)
    cx.xn_sb = alloc("xn_sb", [128, NCH * G], BF16)
    cx.ident_f = alloc("c_idf", [128, 128], F32)
    cx.ident_b = alloc("c_idb", [128, 128], BF16)
    cx.utones = alloc("c_ut", [128, 128], F32)
    cx.masksl = alloc("c_msl", [C, C], F32)
    cx.twoI = alloc("c_2I", [128, C], F32)
    cx.ones_r = alloc("c_1r", [1, 128], F32)
    cx.ones_rb = alloc("c_1rb", [1, 128], BF16)
    cx.two_rb = alloc("c_2rb", [1, 128], BF16)
    cx.ones_c = alloc("c_1c", [128, 1], F32)
    cx.ones_cb = alloc("c_1cb", [128, 1], BF16)
    cx.utones_b = alloc("c_utb", [128, 128], BF16)
    cx.convw = alloc("c_cw", [128, 96], F32)
    cx.gcols = alloc("c_gc", [128, 2], F32)
    cx.b1cols = alloc("c_b1c", [128, 32], F32)
    cx.aimask = alloc("c_aim", [128, NSEQ], F32)
    cx.w_wb = alloc("c_wb", [128, 32], BF16)
    cx.tails = alloc("c_tails", [128, 72], F32)
    cx.cvals = alloc("c_cvals", [128, 4], F32)
    cx.r_bqkv = alloc("r_bqkv", [1, 3 * D], BF16)
    cx.r_bg = alloc("r_bg", [1, D], BF16)
    cx.r_gb = alloc("r_gb", [1, NE], BF16)
    cx.r_bi = alloc("r_bi", [1, DFF], BF16)
    cx.r_bo = alloc("r_bo", [1, D], BF16)
    cx.r_b2 = alloc("r_b2", [1, D], BF16)
    cx.r_carry = alloc("r_carry", [1, D], F32)
    cx.r_sd = alloc("r_sd", [1, D], F32)
    cx.r_sdb = alloc("r_sdb", [1, D], BF16)
    cx.r_rkb = alloc("r_rkb", [1, 4 * 512], BF16)
    cx.r_tmpf = alloc("r_tmpf", [1, 512], F32)
    cx.vs_g = alloc("vs_g", [128, NSUB], F32)
    cx.count_g = alloc("count_g", [128, NSUB * NE], F32)
    cx.qh = alloc("qh", [128, NCH * 512], BF16)
    cx.kh = alloc("kh", [128, NCH * 512], BF16)
    cx.kb = alloc("kb", [128, NCH * 512], BF16)
    cx.vb = alloc("vb", [128, NCH * 512], BF16)

    oA = (off[0] + 63) // 64 * 64
    oB = oA + 49152
    endB = oB + 32832
    assert endB <= base[0] + 208448, f"SBUF over budget: {endB - base[0]}"

    def allocA(name, shape, dtype, o):
        sz = shape[-1] * (2 if dtype == BF16 else 4)
        assert o + sz <= 49152, (name, o, sz)
        return nc.alloc_sbuf_tensor_at(name, list(shape), dtype, offset=oA + o)

    def allocB(name, shape, dtype, o):
        sz = shape[-1] * (2 if dtype == BF16 else 4)
        assert o + sz <= 32832, (name, o, sz)
        return nc.alloc_sbuf_tensor_at(name, list(shape), dtype, offset=oB + o)

    cx.pro_f = allocA("pro_f", [128, DFF], F32, 0)
    cx.pro_b = allocA("pro_b", [128, DFF], BF16, 16384)
    cx.w_qkvT = allocA("w_qkvT", [128, NCH * 3 * D], BF16, 0)
    cx.w_gT = allocB("w_gT", [128, NCH * D], BF16, 0)
    cx.q_f = allocB("q_f", [128, D], F32, 16384)
    cx.kv_b = allocB("kv_b", [128, D], BF16, 20480)
    cx.gate_f = allocB("gate_f", [128, D], F32, 24576)
    cx.t1_f = allocB("t1_f", [128, D], F32, 28672)
    cx.ln_tm = allocB("ln_tm", [128, D], BF16, 16384)
    cx.ln_sq = allocB("ln_sq", [128, D], F32, 20480)
    cx.ln_c = allocB("ln_c", [128, 64], F32, 32512)
    cx.w_apad = allocA("w_apad", [128, NCH * 512], BF16, 0)
    cx.w_gwT = allocA("w_gwT", [128, NCH * NE], BF16, 8192)
    cx.moe_sc = allocA("moe_sc", [128, 40], F32, 8448)
    cx.w_wiTq = allocA("w_wiTq", [128, NCH * FQ], BF16, 0)
    cx.w_bpadq = allocA("w_bpadq", [128, 4 * FQ], BF16, 16384)
    cx.w_woTq = allocA("w_woTq", [128, NCH * D], BF16, 24576)
    cx.tsum_b = [allocA("tsum0", [128, FQ], BF16, 40960),
                 allocA("tsum1", [128, FQ], BF16, 43008),
                 allocA("tsum2", [128, FQ], BF16, 45056),
                 allocA("tsum3", [128, FQ], BF16, 47104)]
    cx.relu_b2 = allocB("relu_b2", [128, FQ], BF16, 18432)
    cx.base_b = allocB("base_b", [128, FQ], BF16, 0)
    cx.base_b2 = allocB("base_b2", [128, FQ], BF16, 26624)
    cx.relu_b = allocB("relu_b", [128, FQ], BF16, 2048)
    cx.acc0_b = allocB("acc0_b", [128, FQ], BF16, 4096)
    cx.acc1_b = allocB("acc1_b", [128, FQ], BF16, 6144)
    cx.acc2_b = allocB("acc2_b", [128, FQ], BF16, 28672)
    cx.acc3_b = allocB("acc3_b", [128, FQ], BF16, 30720)
    cx.accfm_b = allocB("accfm_b", [128, FQ], BF16, 8192)
    cx.w_prj2 = [allocA("w_prj0", [128, NCH * D], BF16, 0),
                 allocA("w_prjB", [128, NCH * D], BF16, 24576)]
    cx.silu_f = allocA("silu_f", [128, NCH * 512], BF16, 16384)
    cx.aug_f = allocA("aug_f", [128, 520], F32, 40960)
    cx.csA_f = allocA("csA_f", [128, 512], F32, 43072)
    cx.csB_f = allocA("csB_f", [128, 512], F32, 45120)
    cx.sq_b = allocA("sq_b", [128, 512], BF16, 47168)
    cx.r_beta = allocB("r_beta", [1, 4 * 512], BF16, 18432)
    cx.r_rq = allocB("r_rq", [1, 4 * 512], BF16, 22528)
    cx.w_wog = allocB("w_wog", [128, NCH * D], BF16, 0)
    cx.tm_set = [
        (allocA("tm_qh0", [128, D], BF16, 0), allocA("tm_kh0", [128, D], BF16, 2048),
         allocA("tm_kb0", [128, D], BF16, 4096), allocA("tm_vb0", [128, D], BF16, 6144)),
        (allocB("tm_qh1", [128, D], BF16, 18432), allocB("tm_kh1", [128, D], BF16, 20480),
         allocB("tm_kb1", [128, D], BF16, 22528), allocB("tm_vb1", [128, D], BF16, 24576)),
    ]
    cx.WU_set = [
        (allocA("W_b0", [128, D], BF16, 8192), allocA("U_b0", [128, D], BF16, 10240)),
        (allocA("W_b1", [128, D], BF16, 12288), allocA("U_b1", [128, D], BF16, 14336)),
    ]
    cx.o_set = [allocA("o_tm0", [128, D], F32, 28672),
                allocB("o_tm1", [128, D], F32, 26624)]
    cx.u_f = allocA("u_f", [128, D], F32, 16384)
    cx.t3_f = allocA("t3_f", [128, D], F32, 20480)
    cx.t3_b = allocB("t3_b", [128, D], BF16, 16384)
    cx.oi_f = allocA("oi_f", [128, D], F32, 24576)
    cx.osq_f = allocA("osq_f", [128, D], F32, 32768)
    cx.on_b = allocA("on_b", [128, D], BF16, 36864)
    cx.ofm_b = allocA("ofm_b", [128, D], BF16, 38912)
    cx.Lm = allocA("Lm", [C, C], F32, 40960)
    cx.LmT = allocA("LmT", [C, C], F32, 41216)
    cx.X_a = allocA("X_a", [C, C], F32, 41472)
    cx.XT_a = allocA("XT_a", [C, C], F32, 41728)
    cx.X_b = allocA("X_b", [C, C], F32, 41984)
    cx.XT_b = allocA("XT_b", [C, C], F32, 42240)
    cx.G_a = allocA("G_a", [C, C], F32, 42496)
    cx.G_b = allocA("G_b", [C, C], F32, 42752)
    cx.G_c = allocA("G_c", [C, C], F32, 43008)
    cx.TT4 = [allocA(f"TT4_{i}", [128, C], BF16, 43264 + 128 * i) for i in range(4)]
    cx.aq_c = allocA("aq_c", [128, 4], F32, 43840)
    cx.ou_f = allocA("ou_f", [128, D], F32, 43904)
    cx.rms_c = allocA("rms_c", [128, 12], F32, 48000)
    cx.w_u1 = allocA("w_u1", [128, NCH * 256], BF16, 0)
    cx.w_v1 = allocA("w_v1", [128, 2 * DFF], BF16, 4096)
    cx.w_u2 = allocA("w_u2", [128, 32 * 256], BF16, 20480)
    cx.w_v2 = allocA("w_v2", [128, 2 * D], BF16, 36864)
    cx.hg = allocB("hg", [128, 4 * G], BF16, 10240)
    cx.P_b = allocB("P_b", [128, 2 * G], BF16, 0)
    cx.H_b = allocB("H_b", [128, G], BF16, 4096)
    cx.Q_b = allocB("Q_b", [128, 2 * G], BF16, 6144)

    cx.ps_a = nc.alloc_psum_tensor("ps_a", [128, 2048], F32)
    cx.ps_b = nc.alloc_psum_tensor("ps_b", [128, 1024], F32)
    cx.ps_c = nc.alloc_psum_tensor("ps_c", [128, 512], F32)
    cx.ps_t = nc.alloc_psum_tensor("ps_t", [128, 1024], BF16)

    emit_consts(cx, stages)
    emit_prologue(cx, stages)
    cx.S.join("prologue")
    for si in range(NSEQ):
        for gh in range(2):
            if DEBUG_GROUPS is not None and (si, gh) not in DEBUG_GROUPS:
                continue
            emit_group(cx, stages, si, gh)
    cx.S.join("end")
    cx.S.finalize()
    return nc


def emit_consts(cx, stages):
    d = cx.dr
    for sb, key in [(cx.ident_f, "ident_f"), (cx.ident_b, "ident_b"),
                    (cx.utones, "utones"), (cx.masksl, "masksl"),
                    (cx.twoI, "twoI"), (cx.ones_r, "ones_r"),
                    (cx.ones_rb, "ones_rb"), (cx.two_rb, "two_rb"),
                    (cx.ones_c, "ones_c"), (cx.ones_cb, "ones_cb"),
                    (cx.utones_b, "utones_b")]:
        DMA(cx, "sync", sb.ap()[:, :], cx.inl[key].ap()[:, :], "const")
    OP(cx, "vector", lambda E: E.memset(cx.cvals.ap()[:, 0:1], 1e-5),
       [], [cx.cvals.ap()[:, 0:1]])
    for name, sb in [("gcols", cx.gcols), ("b1cols", cx.b1cols),
                     ("aimask", cx.aimask)]:
        if name in d:
            DMA(cx, "sync", sb.ap()[:, :], d[name].ap()[:, :])
    if "convw" in d:
        for t in range(3):
            for c in range(NCH):
                o0 = t * 32 + c * 4
                DMA(cx, "sync", cx.convw.ap()[:, o0:o0 + 4],
                    d["convw"].ap()[t, c * 128:(c + 1) * 128, :])


def emit_prologue(cx, stages):
    cx.S.scope = "prologue"
    d, sc = cx.dr, cx.sc
    state = {"i": 0}

    def cast_weight(src_, dst, rows, cols):
        for r0 in range(0, rows, 128):
            i = state["i"] % 2
            state["i"] += 1
            pf = cx.pro_f.ap()[:, i * 2048:i * 2048 + cols] if cols <= 2048 else \
                cx.pro_f.ap()[:, 0:cols]
            pb = cx.pro_b.ap()[:, i * 2048:i * 2048 + cols] if cols <= 2048 else \
                cx.pro_b.ap()[:, 0:cols]
            rr = min(128, rows - r0)
            DMA(cx, "sync", pf[0:rr, :], src_.ap()[r0:r0 + rr, :])
            if i == 0:
                CP(cx, "vector", pb[0:rr, :], pf[0:rr, :])
            else:
                ACT(cx, pb[0:rr, :], pf[0:rr, :], AF.Copy)
            DMA(cx, "gpsimd", dst.ap()[r0:r0 + rr, :], pb[0:rr, :])

    def cast_row(src_, dst, n):
        pf = cx.pro_f.ap()
        DMA(cx, "sync", pf[0:1, 0:n], src_.ap()[:, :])
        CP(cx, "vector", dst.ap()[0:1, 0:n], pf[0:1, 0:n])

    if "l0" in stages:
        cast_weight(d["wqkvT"], sc["b_wqkvT"], D, 3 * D)
        cast_weight(d["wgT"], sc["b_wgT"], D, D)
        cast_weight(d["wiT"], sc["b_wiT"], D, DFF)
        cast_weight(d["woT"], sc["b_woT"], DFF, D)
        cast_weight(d["gwT"], sc["b_gwT"], D, NE)
        cast_row(d["bqkv"], cx.r_bqkv, 3 * D)
        cast_row(d["bg"], cx.r_bg, D)
        cast_row(d["gb"], cx.r_gb, NE)
        cast_row(d["bi"], cx.r_bi, DFF)
        cast_row(d["bo"], cx.r_bo, D)
        for c in range(NCH):
            pb = cx.pro_b.ap()
            OP(cx, "vector", lambda E, pb=pb: E.memset(pb[:, 0:512], 0.0),
               [], [pb[:, 0:512]])
            for p in range(4):
                for el in range(2):
                    e = 2 * p + el
                    dst0 = 128 * p + 64 * el
                    DMA(cx, "sync", cx.pro_f.ap()[:, 0:16],
                        d["lAT"].ap()[e, c * 128:(c + 1) * 128, :])
                    CP(cx, "vector", pb[:, dst0:dst0 + 16], cx.pro_f.ap()[:, 0:16])
            DMA(cx, "gpsimd", sc["b_apad"].ap()[c * 128:(c + 1) * 128, :],
                pb[:, 0:512])
        for p in range(4):
            pb = cx.pro_b.ap()
            OP(cx, "vector", lambda E, pb=pb: E.memset(pb[:, :], 0.0),
               [], [pb[:, :]])
            for el in range(2):
                e = 2 * p + el
                DMA(cx, "sync", cx.pro_f.ap()[0:16, :], d["lBT"].ap()[e, :, :])
                CP(cx, "vector", pb[64 * el:64 * el + 16, :], cx.pro_f.ap()[0:16, :])
            DMA(cx, "gpsimd", sc["b_bpad"].ap()[p, :, :], pb[:, :])
    if "l1" in stages:
        cast_weight(d["Wq"], sc["b_Wq"], D, D)
        cast_weight(d["Wk"], sc["b_Wk"], D, D)
        cast_weight(d["Wv"], sc["b_Wv"], D, D)
        cast_weight(d["u1"], sc["b_u1"], D, 256)
        cast_weight(d["v1"], sc["b_v1"], 256, DFF)
        cast_weight(d["u2"], sc["b_u2"], DFF, 256)
        cast_weight(d["v2"], sc["b_v2"], 256, D)
        cast_row(d["b2"], cx.r_b2, D)
        for c in range(NCH):
            i = c % 2
            pf = cx.pro_f.ap()[:, i * 2048:i * 2048 + D]
            pb = cx.pro_b.ap()[:, i * 2048:i * 2048 + D]
            DMA(cx, "sync", pf, d["Wo"].ap()[c * 128:(c + 1) * 128, :])
            TS_(cx, "vector", pb, pf,
                cx.gcols.ap()[:, c % 2:c % 2 + 1], None, ALU.mult)
            DMA(cx, "gpsimd", sc["b_wog"].ap()[c * 128:(c + 1) * 128, :], pb)
        for c in range(NCH):
            DMA(cx, "sync", cx.pro_f.ap()[:, 0:4], d["Wb"].ap()[c * 128:(c + 1) * 128, :])
            CP(cx, "vector", cx.w_wb.ap()[:, 4 * c:4 * c + 4], cx.pro_f.ap()[:, 0:4])



def emit_group(cx, stages, si, gh):
    gid = si * 2 + gh
    t0 = si * S + gh * G
    xv = cx.dr["x"].ap().rearrange("(a p) d -> p a d", p=128)
    DMA(cx, "gpsimd", cx.x_sb.ap().rearrange("p (a d) -> p a d", d=D),
        xv[:, t0 // 128:t0 // 128 + NSUB, :], "ldx")
    if gh == 0:
        OP(cx, "vector", lambda E: E.memset(cx.r_carry.ap()[:, :], 0.0),
           [], [cx.r_carry.ap()[:, :]])
        OP(cx, "vector", lambda E: E.memset(cx.r_sd.ap()[:, :], 0.0),
           [], [cx.r_sd.ap()[:, :]])
        OP(cx, "vector", lambda E: E.memset(cx.tails.ap()[:, :], 0.0),
           [], [cx.tails.ap()[:, :]])
    cx.S.join(f"ldx{si}{gh}")
    if cx.tap == "x0":
        tap_tm(cx, t0)
        cx.S.join("tapx0")
    if "l0" in stages:
        cx.S.scope = f"ln0_g{gid}"
        emit_ln(cx, t0, "xn0")
        cx.S.join("ln0")
        cx.S.scope = f"l0attn_g{gid}"
        emit_l0attn(cx, si, gh, t0)
        cx.S.join("l0attn")
        cx.S.scope = f"moe_g{gid}"
        emit_moe_gate(cx, t0)
        cx.S.join("moe")
        cx.S.scope = f"l0ffn_g{gid}"
        emit_l0ffn(cx, t0)
        if cx.tap == "x1":
            tap_tm(cx, t0)
            cx.S.join("tapx1")
    if "l1" in stages:
        cx.S.scope = f"ln1_g{gid}"
        emit_ln(cx, t0, "xn1")
        cx.S.join("ln1")
        emit_l1attn(cx, si, gh, t0)
        cx.S.scope = f"l1ffn_g{gid}"
        emit_l1ffn(cx, t0)
        cx.S.join("l1ffn")
        if cx.tap == "x2":
            tap_tm(cx, t0)
            cx.S.join("tapx2")
    ov = cx.dr["out"].ap().rearrange("(a p) d -> p a d", p=128)
    DMA(cx, "gpsimd", ov[:, t0 // 128:t0 // 128 + NSUB, :],
        cx.x_sb.ap().rearrange("p (a d) -> p a d", d=D), "stx")
    cx.S.join(f"stx{si}{gh}")


def tap_tm(cx, t0):
    dv = cx.dr["dbg"].ap().rearrange("(a p) d -> a p d", p=128)
    for s in range(NSUB):
        DMA(cx, "gpsimd", dv[(t0 + s * 128) // 128, :, 0:D],
            cx.x_sb.ap()[:, s * D:(s + 1) * D], "tap")


def tap_fm_bf16(cx, t0, sb, width):
    dv = cx.dr["dbg"].ap().rearrange("(a p) d -> a p d", p=128)
    for c in range(NCH):
        DMA(cx, "gpsimd", dv[(t0 + c * 128) // 128, :, 0:width],
            sb.ap()[:, c * width:(c + 1) * width], "tapf")


def emit_ln(cx, t0, tapname):
    lc = cx.ln_c.ap()
    OP(cx, "vector", lambda E: E.memset(lc[:, 8:16], 0.0), [], [lc[:, 8:16]])
    for s in range(NSUB):
        xs = cx.x_sb.ap()[:, s * D:(s + 1) * D]
        RED(cx, lc[:, s:s + 1], xs, ALU.add)
        ACT(cx, cx.ln_sq.ap()[:, :], xs, AF.Square, accum=lc[:, 8 + s:9 + s])
    TS_(cx, "vector", lc[:, 16:24], lc[:, 0:8], 1.0 / D, None, ALU.mult)
    TS_(cx, "vector", lc[:, 24:32], lc[:, 8:16], 1.0 / D, None, ALU.mult)
    TT_(cx, "vector", lc[:, 32:40], lc[:, 16:24], lc[:, 16:24], ALU.mult)
    TT_(cx, "vector", lc[:, 40:48], lc[:, 24:32], lc[:, 32:40], ALU.subtract)
    ACT(cx, lc[:, 48:56], lc[:, 40:48], AF.Ln, bias=cx.cvals.ap()[:, 0:1])
    ACT(cx, lc[:, 56:64], lc[:, 48:56], AF.Exp, scale=-0.5)
    for s in range(NSUB):
        xs = cx.x_sb.ap()[:, s * D:(s + 1) * D]
        TS_(cx, "vector", cx.ln_tm.ap()[:, :], xs, lc[:, 16 + s:17 + s],
            lc[:, 56 + s:57 + s], ALU.subtract, ALU.mult)
        for c in range(NCH):
            TRN(cx, cx.ps_t.ap()[:, c * 128:(c + 1) * 128],
                cx.ln_tm.ap()[:, c * 128:(c + 1) * 128], cx.ident_b.ap()[:, :])
        dst = cx.xn_sb.ap().rearrange("p (c g) -> p c g", g=G)[:, :, s * 128:(s + 1) * 128]
        OP(cx, "vector", lambda E, dst=dst: E.tensor_copy(
            dst, cx.ps_t.ap().rearrange("p (c n) -> p c n", n=128)),
            [cx.ps_t.ap()[:, :]], [dst])
    if cx.tap == tapname:
        tap_fm_bf16(cx, t0, cx.xn_sb, G)


def emit_l0attn(cx, si, gh, t0):
    DMA(cx, "sync", cx.w_qkvT.ap().rearrange("p (c w) -> p c w", w=3 * D),
        cx.sc["b_wqkvT"].ap().rearrange("(c p) w -> p c w", p=128))
    DMA(cx, "sync", cx.w_gT.ap().rearrange("p (c w) -> p c w", w=D),
        cx.sc["b_wgT"].ap().rearrange("(c p) w -> p c w", p=128))
    for s in range(NSUB):
        xn_s = [cx.xn_sb.ap()[:, c * G + s * 128: c * G + (s + 1) * 128]
                for c in range(NCH)]
        def psn(n):
            return (cx.ps_a.ap()[:, n * 512:(n + 1) * 512] if n < 4
                    else cx.ps_b.ap()[:, (n - 4) * 512:(n - 3) * 512])
        for c in range(NCH):
            for n in range(6):
                w = cx.w_qkvT.ap()[:, c * 3 * D + n * 512: c * 3 * D + (n + 1) * 512]
                MM(cx, psn(n), xn_s[c], w, c == 0, False)
        for n in range(6):
            br = cx.r_bqkv.ap()[0:1, n * 512:(n + 1) * 512]
            MM(cx, psn(n), cx.ones_rb.ap()[:, :], br, False, True)
        ACT(cx, cx.q_f.ap()[:, :], cx.ps_a.ap()[:, 0:1024], AF.Copy)
        ACT(cx, cx.t1_f.ap()[:, :], cx.ps_a.ap()[:, 1024:2048], AF.Copy)
        TT_(cx, "vector", cx.kv_b.ap()[:, :], cx.t1_f.ap()[:, :],
            cx.ps_b.ap()[:, :], ALU.mult)
        for c in range(NCH):
            for n in range(2):
                w = cx.w_gT.ap()[:, c * D + n * 512: c * D + (n + 1) * 512]
                MM(cx, cx.ps_b.ap()[:, n * 512:(n + 1) * 512], xn_s[c], w,
                   c == 0, False)
        for n in range(2):
            br = cx.r_bg.ap()[0:1, n * 512:(n + 1) * 512]
            MM(cx, cx.ps_b.ap()[:, n * 512:(n + 1) * 512],
               cx.ones_rb.ap()[:, :], br, False, True)
        ACT(cx, cx.gate_f.ap()[:, :], cx.ps_b.ap()[:, :], AF.Sigmoid)
        for n in range(2):
            ps = cx.ps_a.ap()[:, 1024 + n * 512: 1024 + (n + 1) * 512]
            kvs = cx.kv_b.ap()[:, n * 512:(n + 1) * 512]
            MM(cx, ps, cx.utones_b.ap()[:, :], kvs, True, False)
            cr = cx.r_carry.ap()[0:1, n * 512:(n + 1) * 512]
            MM(cx, ps, cx.ones_r.ap()[:, :], cr, False, True)
        for n in range(2):
            ps = cx.ps_c.ap()[0:1, 0:512]
            kvs = cx.kv_b.ap()[:, n * 512:(n + 1) * 512]
            MM(cx, ps, cx.ones_cb.ap()[:, :], kvs, True, True)
            cr = cx.r_carry.ap()[0:1, n * 512:(n + 1) * 512]
            TT_(cx, "vector", cr, cr, ps, ALU.add)
        TT_(cx, "vector", cx.t1_f.ap()[:, :], cx.q_f.ap()[:, :],
            cx.ps_a.ap()[:, 1024:2048], ALU.mult)
        TT_(cx, "vector", cx.q_f.ap()[:, :], cx.t1_f.ap()[:, :],
            cx.gate_f.ap()[:, :], ALU.mult)
        xs = cx.x_sb.ap()[:, s * D:(s + 1) * D]
        TT_(cx, "vector", xs, xs, cx.q_f.ap()[:, :], ALU.add)
    if cx.tap == "a0x":
        tap_tm(cx, t0)
        cx.S.join("tapa0")


def emit_moe_gate(cx, t0):
    DMA(cx, "sync", cx.w_gwT.ap().rearrange("p (c w) -> p c w", w=NE),
        cx.sc["b_gwT"].ap().rearrange("(c p) w -> p c w", p=128))
    DMA(cx, "sync", cx.w_apad.ap().rearrange("p (c w) -> p c w", w=512),
        cx.sc["b_apad"].ap().rearrange("(c p) w -> p c w", p=128))
    ms = cx.moe_sc.ap()
    for s in range(NSUB):
        xn_s = [cx.xn_sb.ap()[:, c * G + s * 128: c * G + (s + 1) * 128]
                for c in range(NCH)]
        ps = cx.ps_c.ap()[:, 0:NE]
        for c in range(NCH):
            MM(cx, ps, xn_s[c], cx.w_gwT.ap()[:, c * NE:(c + 1) * NE],
               c == 0, False)
        MM(cx, ps, cx.ones_rb.ap()[:, :], cx.r_gb.ap()[:, :], False, True)
        RED(cx, ms[:, 8:9], ps, ALU.max)
        TS_(cx, "vector", ms[:, 9:10], ms[:, 8:9], -1.0, None, ALU.mult)
        ACT(cx, ms[:, 0:8], ps, AF.Exp, bias=ms[:, 9:10])
        RED(cx, ms[:, 10:11], ms[:, 0:8], ALU.add)
        OP(cx, "vector", lambda E: E.reciprocal(ms[:, 11:12], ms[:, 10:11]),
           [ms[:, 10:11]], [ms[:, 11:12]])
        RED(cx, ms[:, 12:13], ms[:, 0:8], ALU.max)
        TS_(cx, "vector", ms[:, 16:24], ms[:, 0:8], ms[:, 12:13], None, ALU.is_ge)
        STT(cx, "vector", ms[:, 24:32], ms[:, 16:24], -1e9, ms[:, 0:8],
            ALU.mult, ALU.add)
        RED(cx, ms[:, 13:14], ms[:, 24:32], ALU.max)
        TT_(cx, "vector", ms[:, 14:15], ms[:, 12:13], ms[:, 13:14], ALU.add)
        vs = cx.vs_g.ap()[:, s:s + 1]
        TT_(cx, "vector", vs, ms[:, 14:15], ms[:, 11:12], ALU.mult)
        cnt = cx.count_g.ap()[:, s * NE:(s + 1) * NE]
        TS_(cx, "vector", cnt, ms[:, 0:8], ms[:, 13:14], None, ALU.is_ge)
    for p in range(4):
        for c in range(NCH):
            l = cx.w_apad.ap()[:, c * 512 + 128 * p: c * 512 + 128 * (p + 1)]
            for n in range(2):
                r = cx.xn_sb.ap()[:, c * G + n * 512: c * G + (n + 1) * 512]
                MM(cx, cx.ps_b.ap()[:, n * 512:(n + 1) * 512], l, r,
                   c == 0, c == NCH - 1)
        CP(cx, "vector", cx.hg.ap()[:, p * G:(p + 1) * G], cx.ps_b.ap()[:, :])
    if cx.tap == "vs":
        dv = cx.dr["dbg"].ap().rearrange("(a p) d -> a p d", p=128)
        DMA(cx, "gpsimd", dv[t0 // 128, :, 0:NSUB], cx.vs_g.ap()[:, :], "tapv")
        DMA(cx, "gpsimd", dv[t0 // 128, :, 16:16 + NSUB * NE],
            cx.count_g.ap()[:, :], "tapc")
        cx.S.join("tapvs")


def emit_l0ffn(cx, t0):
    for q in range(NQ):
        DMA(cx, "sync", cx.w_wiTq.ap().rearrange("p (c w) -> p c w", w=FQ),
            cx.sc["b_wiT"].ap().rearrange("(c p) w -> p c w", p=128)
            [:, :, q * FQ:(q + 1) * FQ])
        DMA(cx, "sync", cx.w_bpadq.ap().rearrange("p (e w) -> p e w", w=FQ),
            cx.sc["b_bpad"].ap().rearrange("e p w -> p e w")
            [:, :, q * FQ:(q + 1) * FQ])
        DMA(cx, "sync", cx.w_woTq.ap().rearrange("p (c w) -> p c w", w=D),
            cx.sc["b_woT"].ap()[q * FQ:(q + 1) * FQ, :]
            .rearrange("(c p) w -> p c w", p=128))

        def xn_of(s):
            return [cx.xn_sb.ap()[:, c * G + s * 128: c * G + (s + 1) * 128]
                    for c in range(NCH)]

        def base_mm(s):
            xn_s = xn_of(s)
            off = 1024 * (s % 2)
            for c in range(NCH):
                for n in range(2):
                    w = cx.w_wiTq.ap()[:, c * FQ + n * 512: c * FQ + (n + 1) * 512]
                    MM(cx, cx.ps_a.ap()[:, off + n * 512: off + (n + 1) * 512],
                       xn_s[c], w, c == 0, False)
            for n in range(2):
                br = cx.r_bi.ap()[0:1, q * FQ + n * 512: q * FQ + (n + 1) * 512]
                MM(cx, cx.ps_a.ap()[:, off + n * 512: off + (n + 1) * 512],
                   cx.ones_rb.ap()[:, :], br, False, True)

        def base_evac(s):
            off = 1024 * (s % 2)
            bb = [cx.base_b, cx.base_b2][s % 2]
            CP(cx, "vector", bb.ap()[:, :], cx.ps_a.ap()[:, off:off + 1024])

        def expert(s, e):
            par = s % 2
            off = 1024 * par
            bb = [cx.base_b, cx.base_b2][par]
            accp = [(cx.acc0_b, cx.acc1_b), (cx.acc2_b, cx.acc3_b)][par]
            relu = [cx.relu_b, cx.relu_b2][par]
            p, el = e // 2, e % 2
            hl = cx.hg.ap()[64 * el:64 * el + 64,
                            p * G + s * 128: p * G + (s + 1) * 128]
            for n in range(2):
                ps = cx.ps_a.ap()[:, off + n * 512: off + (n + 1) * 512]
                r = cx.w_bpadq.ap()[64 * el:64 * el + 64,
                                    p * FQ + n * 512: p * FQ + (n + 1) * 512]
                MM(cx, ps, hl, r, True, False)
                rb = bb.ap()[:, n * 512:(n + 1) * 512]
                MM(cx, ps, cx.ident_b.ap()[:, :], rb, False, True)
            cntc = cx.count_g.ap()[:, s * NE + e: s * NE + e + 1]
            outw = accp[0] if e == 0 else relu
            ACT(cx, outw.ap()[:, :], cx.ps_a.ap()[:, off:off + 1024], AF.Relu,
                scale=cntc)
            if e > 0:
                prev = accp[(e - 1) % 2] if e > 1 else accp[0]
                cur = accp[e % 2]
                TT_(cx, "vector" if par == 0 else "gpsimd", cur.ap()[:, :],
                    prev.ap()[:, :], relu.ap()[:, :], ALU.add)

        def phaseB(s):
            accfin = [(cx.acc0_b, cx.acc1_b), (cx.acc2_b, cx.acc3_b)][s % 2][(NE - 1) % 2]
            for c in range(NCH):
                TRN(cx, cx.ps_t.ap()[:, c * 128:(c + 1) * 128],
                    accfin.ap()[:, c * 128:(c + 1) * 128], cx.ident_b.ap()[:, :])
            ACT(cx, cx.accfm_b.ap()[:, :], cx.ps_t.ap()[:, :], AF.Copy)
            for c in range(NCH):
                l = cx.accfm_b.ap()[:, c * 128:(c + 1) * 128]
                for n in range(2):
                    r = cx.w_woTq.ap()[:, c * D + n * 512: c * D + (n + 1) * 512]
                    MM(cx, cx.ps_b.ap()[:, n * 512:(n + 1) * 512], l, r,
                       c == 0, (c == NCH - 1) and not (q == 0))
            if q == 0:
                for n in range(2):
                    br = cx.r_bo.ap()[0:1, n * 512:(n + 1) * 512]
                    MM(cx, cx.ps_b.ap()[:, n * 512:(n + 1) * 512],
                       cx.two_rb.ap()[:, :], br, False, True)
            xs = cx.x_sb.ap()[:, s * D:(s + 1) * D]
            vs = cx.vs_g.ap()[:, s:s + 1]
            STT(cx, "vector", xs, cx.ps_b.ap()[:, :], vs, xs, ALU.mult, ALU.add)

        for k in range(NSUB // 2):
            s0, s1 = 2 * k, 2 * k + 1
            base_mm(s0)
            base_evac(s0)
            base_mm(s1)
            base_evac(s1)
            for e in range(NE):
                expert(s0, e)
                expert(s1, e)
            phaseB(s0)
            phaseB(s1)
        cx.S.join(f"ffnq{q}")


def emit_l1attn(cx, si, gh, t0):
    gid = si * 2 + gh
    DMA(cx, "sync", cx.w_wog.ap().rearrange("p (c w) -> p c w", w=D),
        cx.sc["b_wog"].ap().rearrange("(c p) w -> p c w", p=128))
    for half in range(2):
        cx.S.scope = f"l1proj_g{gid}h{half}"
        emit_l1_proj(cx, half)
        cx.S.join(f"l1p{half}")
        cx.S.scope = f"l1scan_g{gid}h{half}"
        emit_l1_scan(cx, si, half)
        cx.S.join(f"l1s{half}")
    if cx.tap in ("qh", "kh", "kb", "vb"):
        tap_fm_bf16(cx, t0, getattr(cx, cx.tap), 512)
        cx.S.join("tapl1")


def emit_l1_proj(cx, half):
    ht = half * 512
    for h in range(4):
        ps = cx.ps_c.ap()[0:1, 0:512]
        for c in range(NCH):
            l = cx.w_wb.ap()[:, 4 * c + h: 4 * c + h + 1]
            r = cx.xn_sb.ap()[:, c * G + ht: c * G + ht + 512]
            MM(cx, ps, l, r, c == 0, c == NCH - 1)
        ACT(cx, cx.r_tmpf.ap()[:, :], ps, AF.Tanh, scale=0.5)
        dst = cx.r_beta.ap()[0:1, h * 512:(h + 1) * 512]
        TS_(cx, "vector", dst, cx.r_tmpf.ap()[:, :], 0.5, 0.5, ALU.mult, ALU.add)
    for ti, wname in [(0, "b_Wq"), (1, "b_Wk"), (2, "b_Wv")]:
        wp = cx.w_prj2[ti % 2]
        DMA(cx, "sync", wp.ap().rearrange("p (c w) -> p c w", w=D),
            cx.sc[wname].ap().rearrange("(c p) w -> p c w", p=128),
            qtag=["qsync", "qsync2"][ti % 2])
        for st4 in range(2):
            for dc in range(NCH):
                r = cx.xn_sb.ap()[:, dc * G + ht: dc * G + ht + 512]
                for k4 in range(4):
                    fc = st4 * 4 + k4
                    l = wp.ap()[:, dc * D + fc * 128: dc * D + (fc + 1) * 128]
                    MM(cx, cx.ps_a.ap()[:, k4 * 512:(k4 + 1) * 512], l, r,
                       dc == 0, dc == NCH - 1)
            for k4 in range(4):
                fc = st4 * 4 + k4
                emit_conv_silu(cx, ti, fc, cx.ps_a.ap()[:, k4 * 512:(k4 + 1) * 512])
        emit_l1_postproj(cx, ti, half)


def emit_conv_silu(cx, ti, fc, ps):
    tcol = (ti * NCH + fc) * 3
    tail = cx.tails.ap()[:, tcol:tcol + 3]
    w3 = cx.convw.ap()[:, ti * 32 + fc * 4 + 3: ti * 32 + fc * 4 + 4]
    w2 = cx.convw.ap()[:, ti * 32 + fc * 4 + 2: ti * 32 + fc * 4 + 3]
    w1 = cx.convw.ap()[:, ti * 32 + fc * 4 + 1: ti * 32 + fc * 4 + 2]
    w0 = cx.convw.ap()[:, ti * 32 + fc * 4 + 0: ti * 32 + fc * 4 + 1]
    aug = cx.aug_f.ap()
    CP(cx, "vector", aug[:, 0:3], tail)
    CP(cx, "vector", aug[:, 3:515], ps)
    CP(cx, "vector", tail, aug[:, 512:515])
    TS_(cx, "vector", cx.csA_f.ap()[:, :], ps, w3, None, ALU.mult)
    STT(cx, "vector", cx.csB_f.ap()[:, :], aug[:, 2:514], w2,
        cx.csA_f.ap()[:, :], ALU.mult, ALU.add)
    STT(cx, "vector", cx.csA_f.ap()[:, :], aug[:, 1:513], w1,
        cx.csB_f.ap()[:, :], ALU.mult, ALU.add)
    STT(cx, "vector", cx.csB_f.ap()[:, :], aug[:, 0:512], w0,
        cx.csA_f.ap()[:, :], ALU.mult, ALU.add)
    dst = cx.silu_f.ap()[:, fc * 512:(fc + 1) * 512]
    ACT(cx, dst, cx.csB_f.ap()[:, :], AF.Silu)


def emit_l1_postproj(cx, ti, half):
    sil = cx.silu_f.ap()
    if ti in (0, 1):
        for h in range(4):
            psr = cx.ps_c.ap()[0:1, 0:512]
            for j in range(2):
                c = 2 * h + j
                ACT(cx, cx.sq_b.ap()[:, :], sil[:, c * 512:(c + 1) * 512],
                    AF.Square)
                MM(cx, psr, cx.ones_cb.ap()[:, :], cx.sq_b.ap()[:, :],
                   j == 0, j == 1)
            ACT(cx, cx.r_tmpf.ap()[:, :], psr, AF.Ln)
            dst = cx.r_rq.ap()[0:1, h * 512:(h + 1) * 512]
            ACT(cx, dst, cx.r_tmpf.ap()[:, :], AF.Exp, scale=-0.5)
        if ti == 1:
            for h in range(4):
                TT_(cx, "vector", cx.r_rkb.ap()[0:1, h * 512:(h + 1) * 512],
                    cx.r_rq.ap()[0:1, h * 512:(h + 1) * 512],
                    cx.r_beta.ap()[0:1, h * 512:(h + 1) * 512], ALU.mult)
    outs = [(cx.qh, cx.r_rq), (cx.kh, cx.r_rq), (cx.vb, cx.r_beta)][ti]
    out_sb, rows = outs
    for h in range(4):
        psb = cx.ps_b.ap()[:, 0:512]
        MM(cx, psb, cx.ones_rb.ap()[:, :], rows.ap()[0:1, h * 512:(h + 1) * 512],
           True, True)
        for j in range(2):
            c = 2 * h + j
            TT_(cx, "vector", out_sb.ap()[:, c * 512:(c + 1) * 512],
                sil[:, c * 512:(c + 1) * 512], psb, ALU.mult)
        if ti == 1:
            psb2 = cx.ps_b.ap()[:, 512:1024]
            MM(cx, psb2, cx.ones_rb.ap()[:, :],
               cx.r_rkb.ap()[0:1, h * 512:(h + 1) * 512], True, True)
            for j in range(2):
                c = 2 * h + j
                TT_(cx, "vector", cx.kb.ap()[:, c * 512:(c + 1) * 512],
                    sil[:, c * 512:(c + 1) * 512], psb2, ALU.mult)


def emit_l1_scan(cx, si, half):
    msk = cx.masksl.ap()[:, :]
    id64f = cx.ident_f.ap()[0:C, 0:C]

    def bulk(p):
        sg = p % 2
        tmq, tmk, tmkb, tmvb = cx.tm_set[sg]
        for fm, tm in [(cx.qh, tmq), (cx.kh, tmk), (cx.kb, tmkb), (cx.vb, tmvb)]:
            for c in range(NCH):
                TRN(cx, cx.ps_t.ap()[:, c * 128:(c + 1) * 128],
                    fm.ap()[:, c * 512 + p * 128: c * 512 + (p + 1) * 128],
                    cx.ident_b.ap()[:, :])
            CP(cx, "vector", tm.ap()[:, :], cx.ps_t.ap()[:, :])
        for ci in range(2):
            b0 = C * ci
            psA = cx.ps_c.ap()[0:C, 0:C]
            for c in range(NCH):
                l = cx.kb.ap()[:, c * 512 + p * 128 + b0: c * 512 + p * 128 + b0 + C]
                r = cx.kh.ap()[:, c * 512 + p * 128 + b0: c * 512 + p * 128 + b0 + C]
                MM(cx, psA, l, r, c == 0, c == NCH - 1)
            TT_(cx, "vector", cx.Lm.ap()[:, :], psA, msk, ALU.mult)
            ps1 = cx.ps_a.ap()[0:C, 0:C]
            TRN(cx, ps1, cx.Lm.ap()[:, :], id64f)
            CP(cx, "vector", cx.LmT.ap()[:, :], ps1)
            STT(cx, "vector", cx.G_a.ap()[:, :], cx.LmT.ap()[:, :], -1.0,
                id64f, ALU.mult, ALU.add)
            X, XT = cx.Lm, cx.LmT
            Xn = [cx.X_a, cx.X_b]
            XTn = [cx.XT_a, cx.XT_b]
            Gs = [cx.G_a, cx.G_b, cx.G_c]
            gi = 0
            for lvl in range(5):
                ps_sq = cx.ps_c.ap()[0:C, 64:128]
                ps_sq2 = cx.ps_a.ap()[0:C, 512:576]
                ps_pr = cx.ps_a.ap()[0:C, 1024:1088]
                MM(cx, ps_sq, XT.ap()[0:C, :], X.ap()[0:C, :], True, True)
                CP(cx, "vector", Xn[lvl % 2].ap()[:, :], ps_sq)
                X2 = Xn[lvl % 2]
                if lvl < 4:
                    MM(cx, ps_sq2, X.ap()[0:C, :], XT.ap()[0:C, :], True, True)
                    CP(cx, "vector", XTn[lvl % 2].ap()[:, :], ps_sq2)
                G = Gs[gi % 3]
                Gnew = Gs[(gi + 1) % 3]
                MM(cx, ps_pr, X2.ap()[0:C, :], G.ap()[0:C, :], True, True)
                TT_(cx, "vector", Gnew.ap()[:, :], G.ap()[:, :], ps_pr, ALU.add)
                gi += 1
                X, XT = Xn[lvl % 2], XTn[lvl % 2]
            Gfin = Gs[gi % 3]
            TTp = cx.TT4[p]
            if ci == 0:
                STT(cx, "vector", TTp.ap()[0:C, :], Gfin.ap()[:, :], 3.0,
                    cx.twoI.ap()[0:C, :], ALU.mult, ALU.subtract)
            else:
                psg = cx.ps_a.ap()[64:128, 1536:1600]
                MM(cx, psg, id64f, Gfin.ap()[0:C, :], True, True, tp=(0, 64))
                STT(cx, "vector", TTp.ap()[64:128, :], psg, 3.0,
                    cx.twoI.ap()[64:128, :], ALU.mult, ALU.subtract)
            TTl = TTp.ap()[b0:b0 + C, :]
            Wb_, Ub_ = cx.WU_set[sg]
            for n in range(2):
                MM(cx, cx.ps_a.ap()[b0:b0 + C, n * 512:(n + 1) * 512], TTl,
                   tmkb.ap()[b0:b0 + C, n * 512:(n + 1) * 512], True, True,
                   tp=(None if ci == 0 else (64, 64)))
                MM(cx, cx.ps_a.ap()[b0:b0 + C, 1024 + n * 512:1024 + (n + 1) * 512],
                   TTl, tmvb.ap()[b0:b0 + C, n * 512:(n + 1) * 512], True, True,
                   tp=(None if ci == 0 else (64, 64)))
            ACT(cx, Wb_.ap()[b0:b0 + C, :], cx.ps_a.ap()[b0:b0 + C, 0:1024], AF.Copy)
            ACT(cx, Ub_.ap()[b0:b0 + C, :], cx.ps_a.ap()[b0:b0 + C, 1024:2048],
                AF.Copy)

    def seq(p):
        sg = p % 2
        tmq, tmk, tmkb, tmvb = cx.tm_set[sg]
        Wb_, Ub_ = cx.WU_set[sg]
        otm = cx.o_set[sg]
        for ci in range(2):
            b0 = C * ci
            tp = None if ci == 0 else (0, 64)
            CP(cx, "vector", cx.r_sdb.ap()[0:1, :], cx.r_sd.ap()[0:1, :])
            for n in range(2):
                MM(cx, cx.ps_b.ap()[b0:b0 + C, n * 512:(n + 1) * 512],
                   cx.ones_rb.ap()[0:1, 0:C],
                   cx.r_sdb.ap()[0:1, n * 512:(n + 1) * 512], True, True, tp=tp)
            sdb = cx.ps_b.ap()[b0:b0 + C, 0:1024]
            TT_(cx, "vector", cx.ou_f.ap()[b0:b0 + C, :],
                Wb_.ap()[b0:b0 + C, :], sdb, ALU.mult)
            TT_(cx, "vector", cx.u_f.ap()[b0:b0 + C, :],
                Ub_.ap()[b0:b0 + C, :], cx.ou_f.ap()[b0:b0 + C, :], ALU.subtract)
            TT_(cx, "vector", cx.oi_f.ap()[b0:b0 + C, :],
                tmq.ap()[b0:b0 + C, :], sdb, ALU.mult)
            TT_(cx, "vector", cx.t3_f.ap()[b0:b0 + C, :],
                tmq.ap()[b0:b0 + C, :], tmk.ap()[b0:b0 + C, :], ALU.mult)
            RED(cx, cx.aq_c.ap()[b0:b0 + C, 0:1], cx.t3_f.ap()[b0:b0 + C, :],
                ALU.add)
            TT_(cx, "vector", cx.aq_c.ap()[b0:b0 + C, 1:2],
                cx.aq_c.ap()[b0:b0 + C, 0:1],
                cx.aimask.ap()[b0:b0 + C, si:si + 1], ALU.mult)
            STT(cx, "vector", otm.ap()[b0:b0 + C, :], cx.u_f.ap()[b0:b0 + C, :],
                cx.aq_c.ap()[b0:b0 + C, 1:2], cx.oi_f.ap()[b0:b0 + C, :],
                ALU.mult, ALU.add)
            TT_(cx, "vector", cx.t3_b.ap()[b0:b0 + C, :],
                cx.tm_set[sg][1].ap()[b0:b0 + C, :], cx.u_f.ap()[b0:b0 + C, :],
                ALU.mult)
            for n in range(2):
                psc = cx.ps_b.ap()[0:1, n * 512:(n + 1) * 512]
                MM(cx, psc, cx.ones_cb.ap()[b0:b0 + C, 0:1],
                   cx.t3_b.ap()[b0:b0 + C, n * 512:(n + 1) * 512], True, True)
            sdr = cx.r_sd.ap()[0:1, 0:1024]
            TT_(cx, "vector", sdr, sdr, cx.ps_b.ap()[0:1, 0:1024], ALU.add)

    def ophase(p):
        sg = p % 2
        otm = cx.o_set[sg]
        ACT(cx, cx.osq_f.ap()[:, :], otm.ap()[:, :], AF.Square)
        for h in range(4):
            RED(cx, cx.rms_c.ap()[:, h:h + 1],
                cx.osq_f.ap()[:, h * 256:(h + 1) * 256], ALU.add)
        ACT(cx, cx.rms_c.ap()[:, 4:8], cx.rms_c.ap()[:, 0:4], AF.Ln,
            scale=1.0 / 256, bias=cx.cvals.ap()[:, 0:1])
        ACT(cx, cx.rms_c.ap()[:, 8:12], cx.rms_c.ap()[:, 4:8], AF.Exp, scale=-0.5)
        for h in range(4):
            TS_(cx, "vector", cx.on_b.ap()[:, h * 256:(h + 1) * 256],
                otm.ap()[:, h * 256:(h + 1) * 256],
                cx.rms_c.ap()[:, 8 + h:9 + h], None, ALU.mult)
        for c in range(NCH):
            TRN(cx, cx.ps_t.ap()[:, c * 128:(c + 1) * 128],
                cx.on_b.ap()[:, c * 128:(c + 1) * 128], cx.ident_b.ap()[:, :])
        CP(cx, "vector", cx.ofm_b.ap()[:, :], cx.ps_t.ap()[:, :])
        for c in range(NCH):
            l = cx.ofm_b.ap()[:, c * 128:(c + 1) * 128]
            for n in range(2):
                r = cx.w_wog.ap()[:, c * D + n * 512: c * D + (n + 1) * 512]
                MM(cx, cx.ps_b.ap()[:, n * 512:(n + 1) * 512], l, r,
                   c == 0, c == NCH - 1)
        sidx = half * 4 + p
        xs = cx.x_sb.ap()[:, sidx * D:(sidx + 1) * D]
        TT_(cx, "vector", xs, xs, cx.ps_b.ap()[:, :], ALU.add)

    bulk(0)
    bulk(1)
    seq(0)
    bulk(2)
    ophase(0)
    seq(1)
    bulk(3)
    ophase(1)
    seq(2)
    ophase(2)
    seq(3)
    ophase(3)


def emit_l1ffn(cx, t0):
    DMA(cx, "sync", cx.w_u1.ap().rearrange("p (c w) -> p c w", w=256),
        cx.sc["b_u1"].ap().rearrange("(c p) w -> p c w", p=128))
    DMA(cx, "sync", cx.w_v1.ap().rearrange("p (c w) -> p c w", w=DFF),
        cx.sc["b_v1"].ap().rearrange("(c p) w -> p c w", p=128))
    DMA(cx, "sync", cx.w_v2.ap().rearrange("p (c w) -> p c w", w=D),
        cx.sc["b_v2"].ap().rearrange("(c p) w -> p c w", p=128))
    DMA(cx, "sync", cx.w_u2.ap().rearrange("p (c w) -> p c w", w=256),
        cx.sc["b_u2"].ap().rearrange("(c p) w -> p c w", p=128))
    for rc in range(2):
        for n in range(2):
            ps = cx.ps_b.ap()[:, n * 512:(n + 1) * 512]
            for dc in range(NCH):
                l = cx.w_u1.ap()[:, dc * 256 + rc * 128: dc * 256 + (rc + 1) * 128]
                r = cx.xn_sb.ap()[:, dc * G + n * 512: dc * G + (n + 1) * 512]
                MM(cx, ps, l, r, dc == 0, dc == NCH - 1)
            CP(cx, "vector", cx.P_b.ap()[:, rc * G + n * 512: rc * G + (n + 1) * 512], ps)

    def v2_sub(s):
        for n in range(2):
            ps = cx.ps_b.ap()[:, n * 512:(n + 1) * 512]
            for rc in range(2):
                l = cx.Q_b.ap()[:, rc * G + s * 128: rc * G + (s + 1) * 128]
                r = cx.w_v2.ap()[:, rc * D + n * 512: rc * D + (n + 1) * 512]
                MM(cx, ps, l, r, rc == 0, False)
            br = cx.r_b2.ap()[0:1, n * 512:(n + 1) * 512]
            MM(cx, ps, cx.ones_rb.ap()[:, :], br, False, True)
        xs = cx.x_sb.ap()[:, s * D:(s + 1) * D]
        TT_(cx, "vector", xs, xs, cx.ps_b.ap()[:, :], ALU.add)

    for th in range(2):
        t0h = th * 512
        for fc in range(32):
            slot = 1024 + 512 * (fc % 2)
            ps_h = cx.ps_a.ap()[:, slot:slot + 512]
            for rc in range(2):
                l = cx.w_v1.ap()[:, rc * DFF + fc * 128: rc * DFF + (fc + 1) * 128]
                r = cx.P_b.ap()[:, rc * G + t0h: rc * G + t0h + 512]
                MM(cx, ps_h, l, r, rc == 0, rc == 1)
            hb = cx.H_b.ap()[:, 512 * (fc % 2):512 * (fc % 2) + 512]
            ACT(cx, hb, ps_h, AF.Gelu_apprx_tanh, bias=cx.b1cols.ap()[:, fc:fc + 1])
            for rc in range(2):
                ps = cx.ps_a.ap()[:, rc * 512:(rc + 1) * 512]
                l = cx.w_u2.ap()[:, fc * 256 + rc * 128: fc * 256 + (rc + 1) * 128]
                MM(cx, ps, l, hb, fc == 0, fc == 31)
            if th == 1 and fc % 8 == 7:
                v2_sub(fc // 8)
        CP(cx, "vector", cx.Q_b.ap()[:, t0h:t0h + 512], cx.ps_a.ap()[:, 0:512])
        CP(cx, "vector", cx.Q_b.ap()[:, 1024 + t0h:1024 + t0h + 512],
           cx.ps_a.ap()[:, 512:1024])
    for s in range(4, 8):
        v2_sub(s)



def kernel(**inputs):
    cores = list(range(8))
    in_maps_full = [prep_core_inputs(inputs, c) for c in cores]
    if LAUNCH_MODE == "fused":
        nc = build_program(("l0", "l1"), DEBUG_TAP)
        res = run_bass_kernel_spmd(nc, in_maps_full, core_ids=cores)
        outs = [r["out"] for r in res.results]
    else:
        nc0 = build_program(("l0",), DEBUG_TAP)
        m0 = [{k: v for k, v in m.items() if k not in L0_ONLY_DROP}
              for m in in_maps_full]
        r0 = run_bass_kernel_spmd(nc0, m0, core_ids=cores)
        nc1 = build_program(("l1",), DEBUG_TAP)
        m1 = []
        for i, m in enumerate(in_maps_full):
            mm = {k: v for k, v in m.items() if k not in L1_ONLY_DROP}
            mm["x"] = np.ascontiguousarray(r0.results[i]["out"])
            m1.append(mm)
        r1 = run_bass_kernel_spmd(nc1, m1, core_ids=cores)
        outs = [r["out"] for r in r1.results]
    out = np.concatenate([o.reshape(NSEQ, S, D) for o in outs], axis=0)
    return out.astype(np.float32)
